# version 27
# baseline (speedup 1.0000x reference)
import numpy as np
import ml_dtypes
from contextlib import ExitStack

import concourse.bass as bass
import concourse.mybir as mybir
from concourse.ap import AP
from concourse.bass_utils import run_bass_kernel_spmd

F32 = mybir.dt.float32
F16 = mybir.dt.float16
BF16 = mybir.dt.bfloat16
I32 = mybir.dt.int32
U8 = mybir.dt.uint8
ALU = mybir.AluOpType
ACT = mybir.ActivationFunctionType
AX = mybir.AxisListType.X

NCORES = 8
NTOK = 4096
D = 1024
DI = 2816
NT = NTOK // 128
TT = 512
NG = NTOK // TT
DCH = D // 128
ICH = DI // 128
KQW = 1024
QG = 4
NQG = NT // QG

ENGINES = ("sync", "vector", "scalar", "tensor", "gpsimd")


class Plan:
    def __init__(self):
        self.ops = {e: [] for e in ENGINES}
        self.cnt = {}

    def inc(self, sem, n):
        self.cnt[sem] = self.cnt.get(sem, 0) + n
        return self.cnt[sem]

    def val(self, sem):
        return self.cnt.get(sem, 0)

    def op(self, engine, fn, waits=(), incs=()):
        wl = [(s, v) for (s, v) in waits if v > 0]
        il = []
        for s, n in incs:
            self.inc(s, n)
            il.append((s, n))
        self.ops[engine].append((fn, wl, il))

    def dma(self, queue, out, in_, sem, waits=()):
        self.op(queue, lambda eng, o=out, i=in_: eng.dma_start(out=o, in_=i),
                waits=waits, incs=[(sem, 16)])
        return self.val(sem)

    def wait(self, engine, waits):
        self.op(engine, None, waits=waits)


def _bcast_ap(dram_handle, nfree, npart=128):
    a = dram_handle.ap()
    return AP(tensor=a.tensor, offset=a.offset, ap=[[0, npart], [1, nfree]])


def build_program(upto="full"):
    nc = bass.Bass("TRN2", target_bir_lowering=False, debug=False)

    x0 = nc.dram_tensor("x0", [NTOK, D], F32, kind="ExternalInput").ap()
    Rm = nc.dram_tensor("Rm", [D, D], F32, kind="ExternalInput").ap()
    wg_d = nc.dram_tensor("wg", [DI, D], F32, kind="ExternalInput").ap()
    wu_d = nc.dram_tensor("wu", [DI, D], F32, kind="ExternalInput").ap()
    wd_d = nc.dram_tensor("wd", [D, DI], F32, kind="ExternalInput").ap()
    clip_x = nc.dram_tensor("clip_x", [1], F32, kind="ExternalInput")
    clip_d = nc.dram_tensor("clip_d", [1], F32, kind="ExternalInput")
    id32 = nc.dram_tensor("id32", [128, 128], F32, kind="ExternalInput").ap()
    id16 = nc.dram_tensor("id16", [128, 128], F16, kind="ExternalInput").ap()
    idb16 = nc.dram_tensor("idb16", [128, 128], BF16, kind="ExternalInput").ap()
    maskm = nc.dram_tensor("maskm", [128, 896], F32, kind="ExternalInput").ap()
    out = nc.dram_tensor("out", [NTOK, D], F32, kind="ExternalOutput").ap()

    x1b = nc.dram_tensor("x1b", [NTOK, D], F32)
    x2b = nc.dram_tensor("x2b", [NTOK, D], F32)
    hbuf = nc.dram_tensor("hbuf", [ICH, NG, 128, TT], F32)
    wgsp = nc.dram_tensor("wgsp", [128, DCH * DI], BF16)
    wusp = nc.dram_tensor("wusp", [128, DCH * DI], BF16)
    wdsp = nc.dram_tensor("wdsp", [128, ICH * D], BF16)
    encTh_d = nc.dram_tensor("encTh_d", [NTOK, D], F16)
    encTl_d = nc.dram_tensor("encTl_d", [NTOK, D], F16)
    encFh_d = nc.dram_tensor("encFh_d", [D, NTOK], F16)
    encFl_d = nc.dram_tensor("encFl_d", [D, NTOK], F16)
    qth_d = nc.dram_tensor("qth_d", [128, NQG * DCH * TT], F16)
    qtl_d = nc.dram_tensor("qtl_d", [128, NQG * DCH * TT], F16)
    cc_in = [nc.dram_tensor(f"cc_in{i}", [1], F32) for i in range(4)]
    cc_out = [nc.dram_tensor(f"cc_out{i}", [NCORES], F32) for i in range(4)]

    p = Plan()
    es = ExitStack()
    with es:
        sbt = lambda name, shape, dt: es.enter_context(nc.sbuf_tensor(name, shape, dt))

        c_id32 = sbt("c_id32", [128, 128], F32)
        c_id16 = sbt("c_id16", [128, 128], F16)
        c_idb16 = sbt("c_idb16", [128, 128], BF16)
        c_ones16 = sbt("c_ones16", [128, 128], F16)
        c_mask = sbt("c_mask", [128, 896], F32)
        sig_x = sbt("sig_x", [128, 1], F32)
        sig_d = sbt("sig_d", [128, 1], F32)
        c127 = sbt("c127", [128, 1], F32)
        r127 = sbt("r127", [128, 1], F32)
        eps_c = sbt("eps_c", [128, 1], F32)
        c256 = sbt("c256", [128, 1], F32)
        sWg = sbt("sWg", [128, ICH], F32)
        sWu = sbt("sWu", [128, ICH], F32)
        sWd = sbt("sWd", [128, DCH], F32)
        ag = sbt("ag", [128, ICH], F32)
        au = sbt("au", [128, ICH], F32)
        adh = sbt("adh", [128, DCH], F32)
        m_col = sbt("m_col", [128, NT], F32)
        nm_col = sbt("nm_col", [128, NT], F32)
        rs_col = sbt("rs_col", [128, NT], F32)
        krx_col = sbt("krx_col", [128, NT], F32)
        amax_run = sbt("amax_run", [128, 1], F32)
        t0c = sbt("t0c", [128, NT], F32)
        t1c = sbt("t1c", [128, NT], F32)
        habs_run = sbt("habs_run", [128, 1], F32)
        habs_tmp = sbt("habs_tmp", [128, 1], F32)
        ccg = sbt("ccg", [128, NCORES], F32)
        sxv = sbt("sxv", [128, 1], F32)
        rxv = sbt("rxv", [128, 1], F32)
        shv = sbt("shv", [128, 1], F32)
        rhv = sbt("rhv", [128, 1], F32)
        t0 = sbt("t0", [128, 1], F32)
        t1 = sbt("t1", [128, 1], F32)
        t3 = sbt("t3", [1, 1], F32)
        lsum_t = sbt("lsum_t", [128, 1], F32)
        lrun = sbt("lrun", [128, NT], F32)
        mrun = sbt("mrun", [128, NT], F32)
        mt = sbt("mt", [128, 1], F32)
        mnew = sbt("mnew", [128, 1], F32)
        nmnew = sbt("nmnew", [128, 1], F32)
        fscb = [sbt(f"fsc{i}", [128, 1], F32) for i in range(2)]
        linv = sbt("linv", [128, 1], F32)

        SLOT = 49152
        arenaA = sbt("arenaA", [128, 2 * SLOT], U8)
        aA = nc.lookup_mloc(arenaA).addr

        def alias(name, shape, dt, byte_off, base=None):
            h = nc.virtual_sbuf_tensor(name, shape, dt)
            nc._tensor(h.name, list(h.shape), h.dtype, type="SB")
            mloc = nc.lookup_mloc(h)
            mloc.allocated = True
            mloc.addr = (aA if base is None else base) + byte_off
            return h

        WgT = alias("WgT", [128, DCH * DI], BF16, 0)
        wtileb = [alias(f"wtile{i}", [128, DI], F32, i * 12288) for i in range(2)]
        wi32b = [alias(f"wi32{i}", [128, DI], I32, 24576 + i * 12288) for i in range(2)]
        wcfb = [alias(f"wcf{i}", [128, DI], F32, 49152 + i * 12288) for i in range(2)]
        wq16b = [alias(f"wq16{i}", [128, DI], BF16, 73728 + i * 8192) for i in range(2)]
        evb = alias("evb", [128, 1024], BF16, 90112)
        WuT = alias("WuT", [128, DCH * DI], BF16, SLOT)
        WdT = alias("WdT", [128, ICH * D], BF16, 0)
        aFhb = [alias(f"aFh{i}", [128, DCH * 512], F16, i * 8192)
                for i in range(2)]
        aFlb = [alias(f"aFl{i}", [128, DCH * 512], F16, 16384 + i * 8192)
                for i in range(2)]
        aThb = [alias(f"aTh{i}", [128, 4 * D], F16, 32768 + i * 8192)
                for i in range(2)]
        aTlb = [alias(f"aTl{i}", [128, 4 * D], F16, 49152 + i * 8192)
                for i in range(2)]
        aQh = alias("aQh", [128, DCH * (QG * 128)], F16, 65536)
        aQl = alias("aQl", [128, DCH * (QG * 128)], F16, 73728)
        aO = alias("aO", [128, QG * D], F32, 81920)
        aRh = alias("aRh", [128, DCH * D], F16, 0)
        aRl = alias("aRl", [128, DCH * D], F16, 16384)
        qhb = [alias(f"qh{i}", [128, ICH * TT], BF16, SLOT + i * 22528)
               for i in range(2)]

        arenaB = sbt("arenaB", [128, 104448], U8)
        aB = nc.lookup_mloc(arenaB).addr
        off = [0]

        def aliasB(name, shape, dt, at=None):
            if at is not None:
                off[0] = at
            nbytes = int(np.prod(shape[1:])) * mybir.dt.size(dt)
            h = alias(name, shape, dt, off[0], base=aB)
            off[0] += (nbytes + 31) & ~31
            return h

        xa = [aliasB(f"xa{i}", [128, D], F32) for i in range(4)]
        xa.append(alias("xa4", [128, D], F32, 16384, base=aB))
        xa.append(alias("xa5", [128, D], F32, 20480, base=aB))
        vq = aliasB("vq", [128, D], F32)
        iq = alias("iq", [128, D], I32, 94208)
        qxtok = [aliasB(f"qxtok{i}", [128, D], BF16) for i in range(4)]
        qxTb = [aliasB(f"qxT{i}", [128, DCH * TT], BF16) for i in range(2)]
        silu_t = aliasB("silu_t", [128, TT], F32)
        h_t = [aliasB(f"h_t{i}", [128, TT], F32) for i in range(3)]
        sqs_at = off[0]
        sqs = aliasB("sqs", [128, D], F32)
        cb = aliasB("cb", [128, D], F32, at=sqs_at)
        down_at = off[0]
        hin = [aliasB(f"hin{i}", [128, TT], F32) for i in range(6)]
        hvw = [aliasB(f"hvw{i}", [128, 2 * TT], F32) for i in range(2)]
        hvb = [hvw[i][:, 0:TT] for i in range(2)]
        hvb2 = [hvw[i][:, TT:2 * TT] for i in range(2)]
        hi32one = aliasB("hi32", [128, 2 * TT], I32)
        hi32b = [hi32one, hi32one]
        hcf1 = aliasB("hcf", [128, 2 * TT], F32)
        hcfb = [hcf1, hcf1]
        dsc = aliasB("dsc", [128, TT], F32)
        xout = aliasB("xout", [128, 4 * D], F32)
        ffn_end = off[0]
        wtile = aliasB("wtile", [128, DI], F32, at=down_at)
        wi32 = aliasB("wi32", [128, DI], I32)
        wcf = aliasB("wcf", [128, DI], F32)
        wq16 = aliasB("wq16", [128, DI], BF16)
        assert off[0] <= 104448, off[0]
        yab = [aliasB(f"ya{i}", [128, D], F32, at=0 if i == 0 else None)
               for i in range(2)]
        ya = yab[0]
        yhb = [aliasB(f"yh{i}", [128, D], F16) for i in range(2)]
        ylb = [aliasB(f"yl{i}", [128, D], F16) for i in range(2)]
        stg_h = aliasB("stg_h", [128, DCH * TT], F16)
        stg_l = aliasB("stg_l", [128, DCH * TT], F16)
        Pb = aliasB("Pb", [128, TT], F32)
        PTh = aliasB("PTh", [128, TT], F16)
        PTl = aliasB("PTl", [128, TT], F16)
        x1t = aliasB("x1t", [128, D], F32)
        x2t = aliasB("x2t", [128, D], F32)
        qsq = aliasB("qsq", [128, TT], F16)
        bnd = aliasB("bnd", [128, TT], F32)
        qevhb = [aliasB(f"qevh{i}", [128, TT], F16) for i in range(2)]
        qevlb = [aliasB(f"qevl{i}", [128, TT], F16) for i in range(2)]
        sq2 = aliasB("sq2", [128, D], F32)
        assert off[0] <= 104448, off[0]

        pA = es.enter_context(nc.psum_tensor("pA", [128, 512], F32))
        pB = es.enter_context(nc.psum_tensor("pB", [128, 512], F32))
        pC = es.enter_context(nc.psum_tensor("pC", [128, 512], F32))
        pD = es.enter_context(nc.psum_tensor("pD", [128, 512], F32))
        pO = es.enter_context(nc.psum_tensor("pO", [128, 1024], F32))
        pFb = es.enter_context(nc.psum_tensor("pFb", [128, 1024], BF16))
        pFh = es.enter_context(nc.psum_tensor("pFh", [128, 1024], F16))

        sem_names = (["dx", "dw", "dw0", "dw1", "wsp", "dg", "ds", "dv", "dv0",
                      "dv1", "dr", "drg", "v", "s", "t", "g", "cc", "o"]
                     + ["eA", "eB", "fA", "fB", "qs0", "qs1"]
                     + [f"xa{i}" for i in range(6)]
                     + [f"hs{i}" for i in range(4)]
                     + [f"hi{i}" for i in range(8)])
        sems = {n: es.enter_context(nc.semaphore(n)) for n in sem_names}

        def TS(eng, outap, inap, s1, s2, op0, op1=None, waits=(), incs=(), accum=None):
            def fn(e, o=outap, i=inap, a=s1, b=s2, x=op0, y=op1, ac=accum):
                kw = {"accum_out": ac} if ac is not None else {}
                if y is None:
                    return e.tensor_scalar(o, i, a, None, x, **kw)
                return e.tensor_scalar(o, i, a, b, x, y, **kw)
            p.op(eng, fn, waits=waits, incs=incs)

        def TT_(eng, outap, a, b, aop, waits=(), incs=()):
            p.op(eng, lambda e, o=outap, x=a, y=b, op=aop: e.tensor_tensor(o, x, y, op),
                 waits=waits, incs=incs)

        def COPY(eng, outap, inap, waits=(), incs=()):
            p.op(eng, lambda e, o=outap, i=inap: e.tensor_copy(o, i),
                 waits=waits, incs=incs)

        def RED(eng, outap, inap, aop, absval=False, waits=(), incs=()):
            p.op(eng, lambda e, o=outap, i=inap, op=aop, a=absval:
                 e.tensor_reduce(o, i, AX, op, apply_absolute_value=a),
                 waits=waits, incs=incs)

        def RECIP(eng, outap, inap, waits=(), incs=()):
            p.op(eng, lambda e, o=outap, i=inap: e.reciprocal(o, i),
                 waits=waits, incs=incs)

        def ACTI(outap, inap, func, bias=0.0, scale=1.0, accum=None, waits=(), incs=()):
            def fn(e, o=outap, i=inap, f=func, b=bias, s=scale, a=accum):
                kw = {"accum_out": a} if a is not None else {}
                return e.activation(o, i, f, bias=b, scale=s, **kw)
            p.op("scalar", fn, waits=waits, incs=incs)

        def STT(eng, outap, in0, scalar, in1, op0, op1, waits=(), incs=()):
            p.op(eng, lambda e, o=outap, a=in0, s=scalar, b=in1, x=op0, y=op1:
                 e.scalar_tensor_tensor(o, a, s, b, x, y), waits=waits, incs=incs)

        def MM(outap, lhsT, rhs, start, stop, waits=(), incs=(), tp=False):
            p.op("tensor", lambda e, o=outap, l=lhsT, r=rhs, st=start, sp=stop, t=tp:
                 e.matmul(o, l, r, start=st, stop=sp, is_transpose=t),
                 waits=waits, incs=incs)

        def TRG(outap, inap, ident, start, stop, waits=(), incs=()):
            MM(outap, inap, ident, start, stop, waits=waits, incs=incs, tp=True)

        def MEMSET(eng, ap_, const, waits=(), incs=()):
            p.op(eng, lambda e, a=ap_, c=const: e.memset(a, c),
                 waits=waits, incs=incs)

        p.dma("sync", c_id32[:], id32, "dx")
        p.dma("sync", c_id16[:], id16, "dx")
        p.dma("sync", c_idb16[:], idb16, "dx")
        p.dma("sync", c_mask[:], maskm, "dx")
        p.dma("sync", sig_x[:], _bcast_ap(clip_x, 1), "dx")
        p.dma("sync", sig_d[:], _bcast_ap(clip_d, 1), "dx")
        init_dx = p.val("dx")
        MEMSET("vector", c_ones16[:], 1.0, incs=[("v", 1)])
        MEMSET("vector", c127[:], 127.0, incs=[("v", 1)])
        MEMSET("vector", eps_c[:], 1e-5, incs=[("v", 1)])
        MEMSET("vector", c256[:], 256.0, incs=[("v", 1)])
        nc.const_aps.register(F32, 256.0, c256[:])
        RECIP("vector", r127[:], c127[:], waits=[("v", p.val("v"))], incs=[("v", 1)])
        ACTI(sig_x[:], sig_x[:], ACT.Sigmoid, waits=[("dx", init_dx)], incs=[("s", 1)])
        ACTI(sig_d[:], sig_d[:], ACT.Sigmoid, incs=[("s", 1)])
        init_s = p.val("s")
        p.wait("tensor", [("dx", init_dx)])
        p.wait("vector", [("dx", init_dx)])
        p.wait("gpsimd", [("dx", init_dx)])

        prep_free = {}

        def prep_weight(src_ap, ntiles, freew, sW_t, spill, base):
            nch = freew // 128
            sp3 = spill.ap().rearrange("p (c n) -> p c n", c=nch)
            ev3 = evb.ap().rearrange("p (c n) -> p c n", c=8)
            tiles = []
            for it in range(ntiles):
                def emit(it=it):
                    gt = base + it
                    pp = gt % 2
                    wtile = wtileb[pp]
                    wi32 = wi32b[pp]
                    wcf = wcfb[pp]
                    wq16 = wq16b[pp]
                    sw_ = f"dw{pp}"
                    ld = p.dma("sync", wtile[:, 0:freew],
                               src_ap[it * 128:(it + 1) * 128, :], sw_,
                               waits=[("v", prep_free.get(gt - 2, 0))])
                    RED("vector", t0[:], wtile[:, 0:freew], ALU.max,
                        absval=True,
                        waits=[(sw_, ld), ("v", p.val("v"))], incs=[("v", 1)])
                    TT_("vector", sW_t[:, it:it + 1], t0[:], r127[:], ALU.mult,
                        waits=[("v", p.val("v"))], incs=[("v", 1)])
                    RECIP("vector", t1[:], sW_t[:, it:it + 1],
                          waits=[("v", p.val("v"))], incs=[("v", 1)])
                    TS("vector", wtile[:, 0:freew], wtile[:, 0:freew], t1[:],
                       256.0, ALU.mult, ALU.add,
                       waits=[("v", p.val("v"))], incs=[("v", 1)])
                    p.op("scalar",
                         lambda e, o=wi32[:, 0:freew], i=wtile[:, 0:freew]:
                         e.copy(o, i),
                         waits=[("v", p.val("v")), ("s", p.val("s"))],
                         incs=[("s", 1)])
                    STT("vector", wcf[:, 0:freew], wi32[:, 0:freew], 0.5,
                        wtile[:, 0:freew], ALU.add, ALU.is_le,
                        waits=[("s", p.val("s")), ("v", p.val("v"))],
                        incs=[("v", 1)])
                    STT("vector", wq16[:, 0:freew], wi32[:, 0:freew], -256.0,
                        wcf[:, 0:freew], ALU.add, ALU.add,
                        waits=[("v", p.val("v"))], incs=[("v", 1)])
                    prep_free[gt] = p.val("v")
                    c = 0
                    while c < nch:
                        nb = min(8, nch - c)
                        t_w = p.val("t")
                        for j in range(nb):
                            TRG(pFb[:, j * 128:(j + 1) * 128],
                                wq16[:, (c + j) * 128:(c + j + 1) * 128],
                                c_idb16[:],
                                start=(j == 0), stop=(j == nb - 1),
                                waits=([("v", p.val("v")), ("t", t_w)]
                                       if j == 0 else ()),
                                incs=([("t", 1)] if j == nb - 1 else ()))
                        COPY("vector", ev3[:, 0:nb, :], pFb[:, 0:nb * 128],
                             waits=[("t", p.val("t")), ("wsp", p.val("wsp"))],
                             incs=[("v", 1)])
                        p.dma("gpsimd",
                              sp3[:, c:c + nb, it * 128:(it + 1) * 128],
                              ev3[:, 0:nb, :], "wsp", waits=[("v", p.val("v"))])
                        c += nb
                tiles.append(emit)
            return tiles

        prep_tiles = (prep_weight(wg_d, ICH, D, sWg, wgsp, 0)
                      + prep_weight(wu_d, ICH, D, sWu, wusp, ICH)
                      + prep_weight(wd_d, DCH, DI, sWd, wdsp, 2 * ICH))

        def preduce_max(in_col):
            TRG(pC[0:1, 0:128], in_col, c_id32[:], start=True, stop=True,
                waits=[("t", p.val("t")), ("v", p.val("v")), ("s", p.val("s"))],
                incs=[("t", 1)])
            RED("vector", t3[:], pC[0:1, 0:128], ALU.max,
                waits=[("t", p.val("t"))], incs=[("v", 1)])

        def collective(idx, out_col):
            st = p.dma("gpsimd", cc_in[idx].ap(), t3[:], "dg",
                       waits=[("v", p.val("v"))])

            def fn(e, i=idx):
                return e.collective_compute(
                    "AllGather", ALU.bypass,
                    replica_groups=[list(range(NCORES))],
                    ins=[cc_in[i].ap().opt()],
                    outs=[cc_out[i].ap().opt()],
                )
            p.op("gpsimd", fn, waits=[("dg", st)], incs=[("cc", 1)])
            ld = p.dma("gpsimd", ccg[:], _bcast_ap(cc_out[idx], NCORES), "dg",
                       waits=[("cc", p.val("cc"))])
            RED("vector", out_col, ccg[:], ALU.max, waits=[("dg", ld)],
                incs=[("v", 1)])

        def ffn_call(x_in, x_out, cidx, wload_gu, pre_waits=(), prep=None):
            xin_t = x_in.rearrange("(n q) d -> n q d", q=128)
            xout_t = x_out.rearrange("(n q) d -> n q d", q=128)
            xin_g = x_in.rearrange("(n q) d -> q n d", q=128)

            pa_done = {}
            ppi = 0
            for nt in range(NT):
                buf = xa[nt % 6]
                sx_ = f"xa{nt % 6}"
                pv, ps_ = pa_done.get(nt - 6, (0, 0))
                w = [("v", pv), ("s", ps_)]
                if nt < 6:
                    w += list(pre_waits)
                ld = p.dma("sync", buf[:], xin_t[nt], sx_, waits=w)
                RED("vector", t0c[:, nt:nt + 1], buf[:], ALU.max,
                    waits=[(sx_, ld), ("v", p.val("v"))], incs=[("v", 1)])
                RED("vector", t1c[:, nt:nt + 1], buf[:], ALU.min, incs=[("v", 1)])
                RED("vector", m_col[:, nt:nt + 1], buf[:], ALU.add, incs=[("v", 1)])
                TS("vector", m_col[:, nt:nt + 1], m_col[:, nt:nt + 1], 1.0 / D,
                   None, ALU.mult, waits=[("v", p.val("v"))], incs=[("v", 1)])
                TS("vector", nm_col[:, nt:nt + 1], m_col[:, nt:nt + 1], -1.0,
                   None, ALU.mult, waits=[("v", p.val("v"))], incs=[("v", 1)])
                ACTI(sqs[:], buf[:], ACT.Square, bias=nm_col[:, nt:nt + 1],
                     scale=1.0, accum=rs_col[:, nt:nt + 1],
                     waits=[("v", p.val("v")), ("s", p.val("s"))],
                     incs=[("s", 1)])
                ACTI(rs_col[:, nt:nt + 1], rs_col[:, nt:nt + 1], ACT.Sqrt,
                     bias=eps_c[:], scale=1.0 / D, waits=[("s", p.val("s"))],
                     incs=[("s", 1)])
                pa_done[nt] = (p.val("v"), p.val("s"))
                if prep:
                    tgt = (nt + 1) * len(prep) // NT
                    while ppi < tgt:
                        prep[ppi]()
                        ppi += 1
            if prep:
                p.dma("sync", WgT[:], wgsp.ap(), "dw",
                      waits=[("wsp", p.val("wsp"))])
                p.dma("sync", WuT[:], wusp.ap(), "dw")
                wload_gu = p.val("dw")
            RECIP("vector", rs_col[:], rs_col[:],
                  waits=[("s", p.val("s")), ("v", p.val("v"))], incs=[("v", 1)])
            TT_("vector", t0c[:], t0c[:], m_col[:], ALU.subtract,
                waits=[("v", p.val("v"))], incs=[("v", 1)])
            TT_("vector", t1c[:], m_col[:], t1c[:], ALU.subtract,
                incs=[("v", 1)])
            TT_("vector", t0c[:], t0c[:], t1c[:], ALU.max,
                waits=[("v", p.val("v"))], incs=[("v", 1)])
            TT_("vector", t0c[:], t0c[:], rs_col[:], ALU.mult,
                waits=[("v", p.val("v"))], incs=[("v", 1)])
            RED("vector", amax_run[:], t0c[:], ALU.max,
                waits=[("v", p.val("v"))], incs=[("v", 1)])
            preduce_max(amax_run[:])
            collective(2 * cidx, t0[:])
            TT_("vector", t0[:], t0[:], sig_x[:], ALU.mult,
                waits=[("v", p.val("v")), ("s", init_s)], incs=[("v", 1)])
            TT_("vector", sxv[:], t0[:], r127[:], ALU.mult,
                waits=[("v", p.val("v"))], incs=[("v", 1)])
            RECIP("vector", rxv[:], sxv[:], waits=[("v", p.val("v"))],
                  incs=[("v", 1)])
            TS("vector", krx_col[:], rs_col[:], rxv[:], None, ALU.mult,
               waits=[("v", p.val("v"))], incs=[("v", 1)])
            TS("vector", ag[:], sWg[:], sxv[:], None, ALU.mult,
               waits=[("v", p.val("v"))], incs=[("v", 1)])
            TS("vector", au[:], sWu[:], sxv[:], None, ALU.mult, incs=[("v", 1)])
            scales_v = p.val("v")

            MEMSET("vector", habs_run[:], 0.0, waits=[("v", p.val("v"))],
                   incs=[("v", 1)])
            wgT3 = WgT.ap().rearrange("p (c n) -> p c n", c=DCH)
            wuT3 = WuT.ap().rearrange("p (c n) -> p c n", c=DCH)
            pFb3 = pFb.ap().rearrange("p (c n) -> p c n", c=DCH)
            base_s = p.val("s")
            sil_done = {}
            stt_done = {}
            mm_done_g = {}
            xq_done = {}
            qdone_t = {}
            last_sp = {i: p.val(f"hs{i}") for i in range(3)}

            def quant_vec(g):
                for j in range(4):
                    nt = g * 4 + j
                    buf = xa[j]
                    sx_ = f"xa{j}"
                    ld = p.dma("sync", buf[:], xin_t[nt], sx_,
                               waits=[("v", scales_v if nt < 4
                                       else xq_done.get(nt - 4, 0))])
                    TS("vector", vq[:], buf[:], m_col[:, nt:nt + 1],
                       krx_col[:, nt:nt + 1], ALU.subtract, ALU.mult,
                       waits=[(sx_, ld), ("v", scales_v)], incs=[("v", 1)])
                    TS("vector", vq[:], vq[:], 127.0, -128.0, ALU.min, ALU.max,
                       waits=[("v", p.val("v"))], incs=[("v", 1)])
                    TS("vector", vq[:], vq[:], 256.0, None, ALU.add,
                       waits=[("v", p.val("v"))], incs=[("v", 1)])
                    p.op("scalar", lambda e, o=iq[:], i=vq[:]: e.copy(o, i),
                         waits=[("v", p.val("v")), ("s", p.val("s"))],
                         incs=[("s", 1)])
                    STT("vector", cb[:], iq[:], 0.5, vq[:], ALU.add, ALU.is_le,
                        waits=[("s", p.val("s"))], incs=[("v", 1)])
                    STT("vector", qxtok[j][:], iq[:], -256.0, cb[:],
                        ALU.add, ALU.add,
                        waits=[("v", p.val("v")),
                               ("t", qdone_t.get(g - 1, 0))], incs=[("v", 1)])
                    xq_done[nt] = p.val("v")

            def transp(g):
                qxT = qxTb[g % 2]
                qxT3 = qxT.ap().rearrange("p (c n) -> p c n", c=DCH)
                for j in range(4):
                    nt = g * 4 + j
                    t_w = p.val("t")
                    for k in range(DCH):
                        TRG(pFb[:, k * 128:(k + 1) * 128],
                            qxtok[j][:, k * 128:(k + 1) * 128], c_idb16[:],
                            start=(k == 0), stop=(k == DCH - 1),
                            waits=([("v", max(xq_done[nt], p.val("v"))),
                                    ("t", max(t_w, mm_done_g.get(g - 2, 0)))]
                                   if k == 0 else ()),
                            incs=([("t", 1)] if k == DCH - 1 else ()))
                    COPY("vector", qxT3[:, :, j * 128:(j + 1) * 128], pFb3,
                         waits=[("t", p.val("t"))], incs=[("v", 1)])
                qdone_t[g] = p.val("t")

            def iloop(g):
                qxT = qxTb[g % 2]
                qx_ready = p.val("v")
                base_v = qx_ready
                for i in range(ICH):
                    gi = g * ICH + i
                    pg = pA if gi % 2 == 0 else pC
                    pu = pB if gi % 2 == 0 else pD
                    for k in range(DCH):
                        MM(pg[:], wgT3[:, k, i * 128:(i + 1) * 128],
                           qxT[:, k * TT:(k + 1) * TT],
                           start=(k == 0), stop=(k == DCH - 1),
                           waits=([("v", qx_ready), ("dw", wload_gu),
                                   ("s", sil_done.get(gi - 2, base_s))]
                                  if k == 0 else ()),
                           incs=([("t", 1)] if k == DCH - 1 else ()))
                    ga_t = p.val("t")
                    for k in range(DCH):
                        MM(pu[:], wuT3[:, k, i * 128:(i + 1) * 128],
                           qxT[:, k * TT:(k + 1) * TT],
                           start=(k == 0), stop=(k == DCH - 1),
                           waits=([("v", stt_done.get(gi - 2, base_v))]
                                  if k == 0 else ()),
                           incs=([("t", 1)] if k == DCH - 1 else ()))
                    up_t = p.val("t")
                    ACTI(silu_t[:], pg[:], ACT.Silu, bias=0.0,
                         scale=ag[:, i:i + 1],
                         waits=[("t", ga_t), ("v", p.val("v"))], incs=[("s", 1)])
                    sil_done[gi] = p.val("s")
                    hb = h_t[gi % 3]
                    STT("vector", hb[:], pu[:], au[:, i:i + 1], silu_t[:],
                        ALU.mult, ALU.mult,
                        waits=[("t", up_t), ("s", p.val("s")),
                               (f"hs{gi % 3}", last_sp[gi % 3])],
                        incs=[("v", 1)])
                    stt_done[gi] = p.val("v")
                    hv_v = p.val("v")
                    RED("vector", habs_tmp[:], hb[:], ALU.max, absval=True,
                        waits=[("v", hv_v)], incs=[("v", 1)])
                    TT_("vector", habs_run[:], habs_run[:], habs_tmp[:], ALU.max,
                        waits=[("v", p.val("v"))], incs=[("v", 1)])
                    p.dma("gpsimd", hbuf.ap()[i, g], hb[:], f"hs{gi % 3}",
                          waits=[("v", hv_v)])
                    last_sp[gi % 3] = p.val(f"hs{gi % 3}")
                mm_done_g[g] = p.val("t")

            quant_vec(0)
            transp(0)
            for g in range(NG):
                if g + 1 < NG:
                    quant_vec(g + 1)
                iloop(g)
                if g + 1 < NG:
                    transp(g + 1)

            preduce_max(habs_run[:])
            collective(2 * cidx + 1, t0[:])
            TT_("vector", t0[:], t0[:], sig_d[:], ALU.mult,
                waits=[("v", p.val("v"))], incs=[("v", 1)])
            TT_("vector", shv[:], t0[:], r127[:], ALU.mult,
                waits=[("v", p.val("v"))], incs=[("v", 1)])
            RECIP("vector", rhv[:], shv[:], waits=[("v", p.val("v"))],
                  incs=[("v", 1)])
            TS("vector", adh[:], sWd[:], shv[:], 0.5, ALU.mult, ALU.mult,
               waits=[("v", p.val("v"))], incs=[("v", 1)])
            hscales_v = p.val("v")

            p.dma("sync", WdT[:], wdsp.ap(), "dw", waits=[("t", p.val("t"))])
            wload_d = p.val("dw")

            wdT3 = WdT.ap().rearrange("p (c n) -> p c n", c=ICH)
            xo4 = xout.ap().rearrange("p (j d) -> p j d", j=4)
            p.wait("sync", [(f"hs{i}", p.val(f"hs{i}")) for i in range(4)])
            dsc_done = {}
            dmm_done = {}
            hin_free = {}
            hv_free = {}
            qh_ready = {}

            def quant_group(g):
                qh = qhb[g % 2]
                qh3 = qh.ap().rearrange("p (c n) -> p c n", c=ICH)
                mm_prev = dmm_done.get(g - 2, p.val("t") if g == 0 else 0)
                for ci in range(ICH // 2):
                    c = 2 * ci
                    gci = g * (ICH // 2) + ci
                    pp = gci % 2
                    p8 = gci % 3
                    buf = hin[2 * p8]
                    buf2 = hin[2 * p8 + 1]
                    hv = hvb[pp]
                    hv2 = hvb2[pp]
                    hi32 = hi32b[pp]
                    hcf = hcfb[pp]
                    sh_ = f"hi{2 * p8}"
                    sh2_ = f"hi{2 * p8 + 1}"
                    ld1 = p.dma("sync", buf[:], hbuf.ap()[c, g], sh_,
                                waits=[("s", hin_free.get((gci - 3, 0), 0))])
                    ld2 = p.dma("sync", buf2[:], hbuf.ap()[c + 1, g], sh2_,
                                waits=[("s", hin_free.get((gci - 3, 1), 0))])
                    ACTI(hv, buf[:], ACT.Copy, bias=256.0, scale=rhv[:],
                         waits=[(sh_, ld1),
                                ("v", hscales_v if g == 0 and ci == 0
                                 else hv_free.get(gci - 2, 0)),
                                ("t", mm_prev)], incs=[("s", 1)])
                    hin_free[(gci, 0)] = p.val("s")
                    ACTI(hv2, buf2[:], ACT.Copy, bias=256.0, scale=rhv[:],
                         waits=[(sh2_, ld2)], incs=[("s", 1)])
                    hin_free[(gci, 1)] = p.val("s")
                    TS("vector", hvw[pp][:], hvw[pp][:], 383.0, 128.0,
                       ALU.min, ALU.max,
                       waits=[("s", p.val("s")), ("v", p.val("v"))],
                       incs=[("v", 1)])
                    p.op("scalar", lambda e, o=hi32[:], i=hvw[pp][:]:
                         e.copy(o, i),
                         waits=[("v", p.val("v"))], incs=[("s", 1)])
                    STT("vector", hcf[:], hi32[:], 0.5, hvw[pp][:], ALU.add,
                        ALU.is_le, waits=[("s", p.val("s")),
                                          ("v", p.val("v"))], incs=[("v", 1)])
                    hv_free[gci] = p.val("v")
                    STT("vector", qh3[:, c:c + 2, :], hi32[:], -256.0, hcf[:],
                        ALU.add, ALU.add,
                        waits=[("v", p.val("v"))], incs=[("v", 1)])
                qh_ready[g] = p.val("v")

            def down_group(g):
                qh = qhb[g % 2]
                qh3 = qh.ap().rearrange("p (c n) -> p c n", c=ICH)
                for j in range(4):
                    xld = p.dma("gpsimd", xo4[:, j, :],
                                xin_g[:, g * 4 + j, :], "drg",
                                waits=([("o", p.val("o"))] if j == 0 else ()))
                tr_done = {}
                for hc in range(DCH):
                    pd_ = pA if hc % 2 == 0 else pC
                    for c in range(ICH):
                        MM(pd_[:], wdT3[:, c, hc * 128:(hc + 1) * 128],
                           qh3[:, c, :], start=(c == 0), stop=(c == ICH - 1),
                           waits=([("v", dsc_done.get((g, hc - 2),
                                        qh_ready[g])),
                                   ("dw", wload_d)] if c == 0 else ()),
                           incs=([("t", 1)] if c == ICH - 1 else ()))
                    dmm = p.val("t")
                    if hc > 0:
                        emit_epi(g, hc - 1, tr_done)
                    tr_done[hc] = dmm
                emit_epi(g, DCH - 1, tr_done)
                dmm_done[g] = p.val("t")
                res_v = p.val("v")
                for j in range(4):
                    p.dma("gpsimd", xout_t[g * 4 + j],
                          xout[:, j * D:(j + 1) * D], "o",
                          waits=[("v", res_v)] if j == 0 else ())

            def emit_epi(g, hc, tr_done):
                pd_ = pA if hc % 2 == 0 else pC
                ptr_ = pB if hc % 2 == 0 else pD
                TS("vector", dsc[:], pd_[:], adh[:, hc:hc + 1], None,
                   ALU.mult, waits=[("t", tr_done[hc]), ("v", p.val("v"))],
                   incs=[("v", 1)])
                dsc_done[(g, hc)] = p.val("v")
                t_w2 = p.val("t")
                for j in range(4):
                    TRG(ptr_[:, j * 128:(j + 1) * 128],
                        dsc[:, j * 128:(j + 1) * 128], c_id32[:],
                        start=(j == 0), stop=(j == 3),
                        waits=([("v", dsc_done[(g, hc)]), ("t", t_w2)]
                               if j == 0 else ()),
                        incs=([("t", 1)] if j == 3 else ()))
                tr_t = p.val("t")
                for j in range(4):
                    TT_("vector",
                        xout[:, j * D + hc * 128: j * D + (hc + 1) * 128],
                        ptr_[:, j * 128:(j + 1) * 128],
                        xout[:, j * D + hc * 128: j * D + (hc + 1) * 128],
                        ALU.add,
                        waits=([("t", tr_t), ("drg", p.val("drg"))]
                               if j == 0 else ()),
                        incs=([("v", 1)] if j == 3 else ()))

            quant_group(0)
            for g in range(NG):
                if g + 1 < NG:
                    quant_group(g + 1)
                down_group(g)
        def attention():
            x1_t = x1b.ap().rearrange("(n q) d -> n q d", q=128)
            x2_t = x2b.ap().rearrange("(n q) d -> n q d", q=128)
            eTh4 = encTh_d.ap().rearrange("(n q) d -> n q d", q=128)
            eTl4 = encTl_d.ap().rearrange("(n q) d -> n q d", q=128)
            eThq = encTh_d.ap().rearrange("(c q) d -> q c d", q=128)
            eTlq = encTl_d.ap().rearrange("(c q) d -> q c d", q=128)
            eFh3 = encFh_d.ap().rearrange("(c q) n -> q c n", q=128)
            eFl3 = encFl_d.ap().rearrange("(c q) n -> q c n", q=128)
            qth2 = qth_d.ap()
            qtl2 = qtl_d.ap()
            stgh3 = stg_h.ap().rearrange("p (c n) -> p c n", c=DCH)
            stgl3 = stg_l.ap().rearrange("p (c n) -> p c n", c=DCH)
            aRh3 = aRh.ap().rearrange("p (c n) -> p c n", c=DCH)
            aRl3 = aRl.ap().rearrange("p (c n) -> p c n", c=DCH)
            pFh3 = pFh.ap().rearrange("p (c n) -> p c n", c=DCH)

            for k in range(DCH):
                ld = p.dma("sync", ya[:], Rm[k * 128:(k + 1) * 128, :], "dr",
                           waits=[("v", p.val("v")), ("t", p.val("t"))])
                COPY("vector", aRh3[:, k, :], ya[:], waits=[("dr", ld)],
                     incs=[("v", 1)])
                TT_("vector", aRl3[:, k, :], ya[:], aRh3[:, k, :], ALU.subtract,
                    waits=[("v", p.val("v"))], incs=[("v", 1)])
            r_ready = p.val("v")

            ya_free = {}
            y_free = {}
            dv_free = {}
            qev_done = {}
            qsp_done = {}
            for grp in range(NG):
                for j in range(4):
                    nt = grp * 4 + j
                    pp = nt % 2
                    ya_ = yab[pp]
                    yh = yhb[pp]
                    yl = ylb[pp]
                    sy_ = f"xa{pp}"
                    w = [("v", ya_free.get(nt - 2, 0)),
                         ("s", p.val("s") if nt < 2 else 0)]
                    if nt == 0:
                        w += [("o", p.val("o")), ("v", p.val("v"))]
                    ld = p.dma("sync", ya_[:], x1_t[nt], sy_, waits=w)
                    RED("vector", m_col[:, nt:nt + 1], ya_[:], ALU.add,
                        waits=[(sy_, ld), ("v", p.val("v"))], incs=[("v", 1)])
                    TS("vector", nm_col[:, nt:nt + 1], m_col[:, nt:nt + 1],
                       -1.0 / D, None, ALU.mult, waits=[("v", p.val("v"))],
                       incs=[("v", 1)])
                    ACTI(sq2[:], ya_[:], ACT.Square, bias=nm_col[:, nt:nt + 1],
                         scale=1.0, accum=rs_col[:, nt:nt + 1],
                         waits=[("v", p.val("v")), ("s", p.val("s"))],
                         incs=[("s", 1)])
                    ACTI(rs_col[:, nt:nt + 1], rs_col[:, nt:nt + 1], ACT.Sqrt,
                         bias=eps_c[:], scale=1.0 / D, waits=[("s", p.val("s"))],
                         incs=[("s", 1)])
                    RECIP("vector", rs_col[:, nt:nt + 1], rs_col[:, nt:nt + 1],
                          waits=[("s", p.val("s")), ("v", p.val("v"))],
                          incs=[("v", 1)])
                    TS("vector", ya_[:], ya_[:], nm_col[:, nt:nt + 1],
                       rs_col[:, nt:nt + 1], ALU.add, ALU.mult,
                       waits=[("v", p.val("v"))], incs=[("v", 1)])
                    COPY("vector", yh[:], ya_[:],
                         waits=[("v", max(p.val("v"), y_free.get(nt - 2, 0))),
                                (f"dv{pp}", dv_free.get(nt - 2, 0))],
                         incs=[("v", 1)])
                    TT_("vector", yl[:], ya_[:], yh[:], ALU.subtract,
                        waits=[("v", p.val("v"))], incs=[("v", 1)])
                    y_v = p.val("v")
                    ya_free[nt] = y_v
                    p.dma("gpsimd", eTh4[nt], yh[:], f"dv{pp}",
                          waits=[("v", y_v)])
                    p.dma("gpsimd", eTl4[nt], yl[:], f"dv{pp}")
                    dv_free[nt] = p.val(f"dv{pp}")
                    t_w = p.val("t")
                    for k in range(DCH):
                        TRG(pFh[:, k * 128:(k + 1) * 128],
                            yh[:, k * 128:(k + 1) * 128], c_id16[:],
                            start=(k == 0), stop=(k == DCH - 1),
                            waits=([("v", y_v), ("t", t_w)] if k == 0 else ()),
                            incs=([("t", 1)] if k == DCH - 1 else ()))
                    COPY("vector", stgh3[:, :, j * 128:(j + 1) * 128], pFh3,
                         waits=[("t", p.val("t")), ("dg", p.val("dg"))],
                         incs=[("v", 1)])
                    t_w = p.val("t")
                    for k in range(DCH):
                        TRG(pFh[:, k * 128:(k + 1) * 128],
                            yl[:, k * 128:(k + 1) * 128], c_id16[:],
                            start=(k == 0), stop=(k == DCH - 1),
                            waits=([("v", p.val("v")), ("t", t_w)] if k == 0
                                   else ()),
                            incs=([("t", 1)] if k == DCH - 1 else ()))
                    COPY("vector", stgl3[:, :, j * 128:(j + 1) * 128], pFh3,
                         waits=[("t", p.val("t"))], incs=[("v", 1)])
                    y_free[nt] = p.val("v")
                stg_v = p.val("v")
                p.dma("gpsimd", eFh3[:, :, grp * TT:(grp + 1) * TT], stgh3, "dg",
                      waits=[("v", stg_v)])
                p.dma("gpsimd", eFl3[:, :, grp * TT:(grp + 1) * TT], stgl3, "dg")
                for ec in range(DCH):
                    pq = pA if ec % 2 == 0 else pC
                    gec = grp * DCH + ec
                    n = 0
                    for (lh, rh) in ((aRh3, stgh3), (aRh3, stgl3), (aRl3, stgh3)):
                        for k in range(DCH):
                            n += 1
                            MM(pq[:], lh[:, k, ec * 128:(ec + 1) * 128],
                               rh[:, k, :], start=(n == 1), stop=(n == 24),
                               waits=([("v", max(stg_v, qev_done.get(
                                             gec - 2, 0))),
                                       ("s", p.val("s") if gec == 0 else 0),
                                       ("t", p.val("t") if gec == 0 else 0)]
                                      if n == 1 else ()),
                               incs=([("t", 1)] if n == 24 else ()))
                    qp_t = p.val("t")
                    qeh = qevhb[ec % 2]
                    qel = qevlb[ec % 2]
                    COPY("vector", qeh[:], pq[:],
                         waits=[("t", qp_t),
                                (f"qs{ec % 2}", qsp_done.get(gec - 2, 0)),
                                ("v", p.val("v"))], incs=[("v", 1)])
                    TT_("vector", qel[:], pq[:], qeh[:], ALU.subtract,
                        waits=[("v", p.val("v"))], incs=[("v", 1)])
                    qev_done[gec] = p.val("v")
                    q_v = p.val("v")
                    qsem = f"qs{ec % 2}"
                    qo = grp * DCH * TT + ec * TT
                    p.dma("scalar", qth2[:, qo:qo + TT],
                          qeh[:], qsem, waits=[("v", q_v)])
                    p.dma("scalar", qtl2[:, qo:qo + TT],
                          qel[:], qsem)
                    qsp_done[gec] = p.val(qsem)
            prepass_dv0 = p.val("dv0")
            prepass_dv1 = p.val("dv1")
            prepass_dg = p.val("dg")
            prepass_qs0 = p.val("qs0")
            prepass_qs1 = p.val("qs1")

            aQh3 = aQh.ap().rearrange("p (c n) -> p c n", c=DCH)
            aQl3 = aQl.ap().rearrange("p (c n) -> p c n", c=DCH)
            MEMSET("vector", lrun[:], 0.0, waits=[("v", p.val("v"))],
                   incs=[("v", 1)])
            MEMSET("vector", mrun[:], -1e30, incs=[("v", 1)])
            s_bank_free = {}
            pt_bank_free = [0]
            pb_free_t = [0]
            s_done_t = {}
            av_done_t = {}
            pt_done_v = {}

            def emit_S(pi, qt, kt, kb, qg, eld, ql, b):
                ps = (pA, pC, pD)[pi % 3]
                sfree = s_bank_free.get(pi % 3, (0, 0))
                aFh_ = aF3[0][b % 2]
                aFl_ = aF3[1][b % 2]
                n = 0
                for (lh, rh) in ((aQh3, aFh_), (aQh3, aFl_),
                                 (aQl3, aFh_)):
                    for k in range(DCH):
                        n += 1
                        MM(ps[:], lh[:, k, qt * 128:(qt + 1) * 128],
                           rh[:, k, 0:512],
                           start=(n == 1), stop=(n == 24),
                           waits=([eld, ("ds", ql),
                                   ("s", sfree[1]), ("v", sfree[0])]
                                  if n == 1 else ()),
                           incs=([("t", 1)] if n == 24 else ()))
                s_done_t[pi] = p.val("t")

            def emit_VSP(pi, qt, kt, kb, qg):
                ps = (pA, pC, pD)[pi % 3]
                Q = qg * QG + qt
                K2 = kb
                s_t = s_done_t[pi]
                if K2 == Q // 4:
                    offm = 128 * Q - 512 * K2
                    TT_("vector", ps[:], ps[:],
                        c_mask[:, 384 - offm:896 - offm], ALU.add,
                        waits=[("t", s_t), ("v", p.val("v"))], incs=[("v", 1)])
                RED("vector", mt[:], ps[:], ALU.max,
                    waits=[("t", s_t), ("v", p.val("v"))], incs=[("v", 1)])
                TT_("vector", mnew[:], mrun[:, Q:Q + 1], mt[:], ALU.max,
                    waits=[("v", p.val("v"))], incs=[("v", 1)])
                TS("vector", nmnew[:], mnew[:], -1.0, None, ALU.mult,
                   waits=[("v", p.val("v"))], incs=[("v", 1)])
                mv = p.val("v")
                ACTI(fsc_s[pi][:], mrun[:, Q:Q + 1], ACT.Exp, bias=nmnew[:],
                     scale=1.0, waits=[("v", mv), ("s", p.val("s"))],
                     incs=[("s", 1)])
                COPY("vector", mrun[:, Q:Q + 1], mnew[:],
                     waits=[("s", p.val("s"))], incs=[("v", 1)])
                ACTI(Pb[:], ps[:], ACT.Exp, bias=nmnew[:], scale=1.0,
                     accum=lsum_t[:],
                     waits=[("v", p.val("v")), ("t", pb_free_t[0])],
                     incs=[("s", 1)])
                s_bank_free[pi % 3] = (p.val("v"), p.val("s"))
                STT("vector", lrun[:, Q:Q + 1], lrun[:, Q:Q + 1], fsc_s[pi][:],
                    lsum_t[:], ALU.mult, ALU.add,
                    waits=[("s", p.val("s"))], incs=[("v", 1)])

            def emit_TRG_mm(pi):
                t_w = p.val("t")
                for j in range(4):
                    TRG(pB[:, j * 128:(j + 1) * 128],
                        Pb[:, j * 128:(j + 1) * 128], c_id32[:],
                        start=(j == 0), stop=(j == 3),
                        waits=([("s", p.val("s")), ("t", t_w),
                                ("v", pt_bank_free[0])]
                               if j == 0 else ()),
                        incs=([("t", 1)] if j == 3 else ()))
                pb_free_t[0] = p.val("t")

            def emit_PT_evac(pi):
                COPY("vector", PTh[:], pB[:],
                     waits=[("t", p.val("t")), ("v", p.val("v"))],
                     incs=[("v", 1)])
                TT_("vector", PTl[:], pB[:], PTh[:], ALU.subtract,
                    waits=[("v", p.val("v"))], incs=[("v", 1)])
                pt_v = p.val("v")
                pt_bank_free[0] = pt_v
                pt_done_v[pi] = pt_v

            def emit_AV(pi, b):
                aTh_ = aT3[0][b % 2]
                aTl_ = aT3[1][b % 2]
                n = 0
                for dh in range(2):
                    for (lh, rh) in ((PTh, aTh_), (PTh, aTl_),
                                     (PTl, aTh_)):
                        for kc in range(4):
                            n += 1
                            MM(pO[:, dh * 512:(dh + 1) * 512],
                               lh[:, kc * 128:(kc + 1) * 128],
                               rh[:, kc, dh * 512:(dh + 1) * 512],
                               start=((n % 12) == 1), stop=((n % 12) == 0),
                               waits=([("v", pt_done_v[pi]), enc_T_ld[b]]
                                      if n == 1 else ()),
                               incs=([("t", 1)] if n == 24 else ()))
                av_done_t[pi] = p.val("t")

            def emit_merge(pi, qt, kb, qg, aO_init):
                STT("vector", aO[:, qt * D:(qt + 1) * D],
                    aO[:, qt * D:(qt + 1) * D], fsc_s[pi][:], pO[:],
                    ALU.mult, ALU.add,
                    waits=[("t", av_done_t[pi]), ("v", aO_init)],
                    incs=[("v", 1)])

            fsc_s = {}
            aF3 = [[b.ap().rearrange("p (c n) -> p c n", c=DCH) for b in aFhb],
                   [b.ap().rearrange("p (c n) -> p c n", c=DCH) for b in aFlb]]
            aT3 = [[b.ap().rearrange("p (c n) -> p c n", c=4) for b in aThb],
                   [b.ap().rearrange("p (c n) -> p c n", c=4) for b in aTlb]]
            blocks = [(qg, kb) for qg in range(NQG) for kb in range(qg + 1)]
            blk_mm_done = {}
            enc_ld = {}
            qld = {}

            blk_s_done = {}
            enc_T_ld = {}

            def prefetch_F(b):
                if b >= len(blocks):
                    return
                qg, kb = blocks[b]
                pe = b % 2
                se = "fA" if pe == 0 else "fB"
                w = [("t", blk_s_done.get(b - 2, 0))]
                if b == 0:
                    w += [("t", p.val("t")), ("v", p.val("v")),
                          ("dg", prepass_dg), ("dv0", prepass_dv0),
                          ("dv1", prepass_dv1)]
                p.dma("sync", aF3[0][pe], eFh3[:, :, kb * 512:(kb + 1) * 512],
                      se, waits=w)
                enc_ld[b] = (se, p.dma("sync", aF3[1][pe],
                                       eFl3[:, :, kb * 512:(kb + 1) * 512],
                                       se))

            def prefetch_T(b):
                if b >= len(blocks):
                    return
                qg, kb = blocks[b]
                pe = b % 2
                se = "eA" if pe == 0 else "eB"
                w = [("t", blk_mm_done.get(b - 2, 0))]
                if b == 0:
                    w += [("t", p.val("t")), ("v", p.val("v")),
                          ("dg", prepass_dg), ("dv0", prepass_dv0),
                          ("dv1", prepass_dv1)]
                p.dma("sync", aT3[0][pe], eThq[:, kb * 4:(kb + 1) * 4, :], se,
                      waits=w)
                enc_T_ld[b] = (se, p.dma("sync", aT3[1][pe],
                                         eTlq[:, kb * 4:(kb + 1) * 4, :],
                                         se))

            aO_inits = {}

            def finalize(qg):
                for qt in range(QG):
                    Q = qg * QG + qt
                    RECIP("vector", linv[:], lrun[:, Q:Q + 1],
                          waits=[("v", p.val("v"))], incs=[("v", 1)])
                    ld = p.dma("sync", x1t[:], x1_t[Q], "dr",
                               waits=[("dv", p.val("dv"))])
                    STT("vector", x2t[:], aO[:, qt * D:(qt + 1) * D], linv[:],
                        x1t[:], ALU.mult, ALU.add,
                        waits=[("dr", ld), ("v", p.val("v")),
                               ("dv", p.val("dv"))], incs=[("v", 1)])
                    p.dma("gpsimd", x2_t[Q], x2t[:], "dv",
                          waits=[("v", p.val("v"))])

            flat = [(b, qg, kb, qt) for b, (qg, kb) in enumerate(blocks)
                    for qt in range(QG)]
            NP = len(flat)
            entered = set()

            def block_entry(b):
                if b in entered:
                    return
                entered.add(b)
                qg, kb = blocks[b]
                if kb == 0:
                    qo = qg * DCH * TT
                    p.dma("scalar", aQh[:],
                          qth2[:, qo:qo + DCH * TT], "ds",
                          waits=[("t", p.val("t")), ("qs0", prepass_qs0),
                                 ("qs1", prepass_qs1)])
                    qld[qg] = p.dma("scalar", aQl[:],
                                    qtl2[:, qo:qo + DCH * TT], "ds")

            def lead_S(j):
                if j >= NP:
                    return
                bj, qgj, kbj, qtj = flat[j]
                block_entry(bj)
                fsc_s[j] = fscb[j % 2]
                emit_S(j, qtj, 0, kbj, qgj, enc_ld[bj], qld[qgj], bj)

            def memset_aO(qg):
                MEMSET("vector", aO[:], 0.0,
                       waits=[("v", p.val("v")), ("t", p.val("t"))],
                       incs=[("v", 1)])
                aO_inits[qg] = p.val("v")

            def post_merge(j):
                bj, qgj, kbj, qtj = flat[j]
                emit_merge(j, qtj, kbj, qgj, aO_inits[qgj])
                if qtj == QG - 1 and kbj == qgj:
                    finalize(qgj)
                    if j + 1 < NP:
                        memset_aO(flat[j + 1][1])

            prefetch_F(0)
            prefetch_T(0)
            prefetch_F(1)
            prefetch_T(1)
            lead_S(0)
            memset_aO(0)
            emit_VSP(0, flat[0][3], 0, flat[0][2], flat[0][1])
            for i in range(NP):
                bi, qgi, kbi, qti = flat[i]
                emit_TRG_mm(i)
                if i > 0:
                    post_merge(i - 1)
                emit_PT_evac(i)
                lead_S(i + 1)
                if i + 2 < NP:
                    block_entry(flat[i + 2][0])
                if i + 1 < NP and flat[i + 1][3] == QG - 1:
                    bn = flat[i + 1][0]
                    blk_s_done[bn] = s_done_t[i + 1]
                    prefetch_F(bn + 2)
                emit_AV(i, bi)
                if qti == QG - 1:
                    blk_mm_done[bi] = p.val("t")
                    prefetch_T(bi + 2)
                if i + 1 < NP:
                    bn, qgn, kbn, qtn = flat[i + 1]
                    emit_VSP(i + 1, qtn, 0, kbn, qgn)
            post_merge(NP - 1)

        ffn_call(x0, x1b.ap(), cidx=0, wload_gu=None,
                 pre_waits=[("dx", init_dx)], prep=prep_tiles)
        ffn1_o = p.val("o")

        if upto == "ffn1":
            for nt in range(NT):
                p.dma("gpsimd",
                      out.rearrange("(n q) d -> n q d", q=128)[nt],
                      x1b.ap().rearrange("(n q) d -> n q d", q=128)[nt],
                      "dg", waits=[("o", ffn1_o)] if nt == 0 else ())
            p.wait("gpsimd", [("dg", p.val("dg"))])
        else:
            attention()
            attn_v = p.val("v")
            attn_t = p.val("t")
            if upto == "attn":
                for nt in range(NT):
                    p.dma("gpsimd",
                          out.rearrange("(n q) d -> n q d", q=128)[nt],
                          x2b.ap().rearrange("(n q) d -> n q d", q=128)[nt],
                          "dg", waits=[("dv", p.val("dv"))] if nt == 0 else ())
                p.wait("gpsimd", [("dg", p.val("dg"))])
            else:
                p.dma("sync", WgT[:], wgsp.ap(), "dw",
                      waits=[("t", attn_t), ("v", attn_v)])
                p.dma("sync", WuT[:], wusp.ap(), "dw")
                wload2 = p.val("dw")
                ffn_call(x2b.ap(), out, cidx=1, wload_gu=wload2,
                         pre_waits=[("dv", p.val("dv")), ("dv0", p.val("dv0")),
                                    ("dv1", p.val("dv1")), ("ds", p.val("ds")),
                                    ("dg", p.val("dg")), ("dr", p.val("dr")),
                                    ("drg", p.val("drg")), ("dx", p.val("dx"))])
                p.wait("gpsimd", [("o", p.val("o"))])

        with nc.Block() as block:
            def emit_for(engine_name):
                def run(eng):
                    for fn, waits, incs in p.ops[engine_name]:
                        for s, v in waits:
                            eng.wait_ge(sems[s], v)
                        if fn is None:
                            continue
                        inst = fn(eng)
                        for s, n in incs:
                            inst.then_inc(sems[s], n)
                return run
            block.sync(emit_for("sync"))
            block.vector(emit_for("vector"))
            block.scalar(emit_for("scalar"))
            block.tensor(emit_for("tensor"))
            block.gpsimd(emit_for("gpsimd"))

    return nc


def _host_inputs(inputs, core):
    x = np.ascontiguousarray(np.asarray(inputs["L0_attn_encoding"])[core])
    m = {
        "x0": x.astype(np.float32, copy=False),
        "Rm": np.asarray(inputs["L0_attn_R"], dtype=np.float32),
        "wg": np.asarray(inputs["L0_ffn_w_gate"], dtype=np.float32),
        "wu": np.asarray(inputs["L0_ffn_w_up"], dtype=np.float32),
        "wd": np.asarray(inputs["L0_ffn_w_down"], dtype=np.float32),
        "clip_x": np.asarray(inputs["L0_ffn_clip_x"], dtype=np.float32),
        "clip_d": np.asarray(inputs["L0_ffn_clip_down"], dtype=np.float32),
        "id32": np.eye(128, dtype=np.float32),
        "id16": np.eye(128, dtype=np.float16),
        "idb16": np.eye(128).astype(ml_dtypes.bfloat16),
    }
    jj = np.arange(896)[None, :]
    pp = np.arange(128)[:, None]
    m["maskm"] = np.where(jj <= pp + 384, 0.0, -1e9).astype(np.float32)
    return m


def kernel(**inputs):
    upto = inputs.pop("_upto", "full")
    nc = build_program(upto=upto)
    in_maps = [_host_inputs(inputs, b) for b in range(NCORES)]
    res = run_bass_kernel_spmd(nc, in_maps, core_ids=list(range(NCORES)))
    return np.stack([np.asarray(res.results[b]["out"]) for b in range(NCORES)],
                    axis=0)


# version 29
# speedup vs baseline: 1.0105x; 1.0105x over previous
import numpy as np
import ml_dtypes
from contextlib import ExitStack

import concourse.bass as bass
import concourse.mybir as mybir
from concourse.ap import AP
from concourse.bass_utils import run_bass_kernel_spmd

F32 = mybir.dt.float32
F16 = mybir.dt.float16
BF16 = mybir.dt.bfloat16
I32 = mybir.dt.int32
U8 = mybir.dt.uint8
ALU = mybir.AluOpType
ACT = mybir.ActivationFunctionType
AX = mybir.AxisListType.X

NCORES = 8
NTOK = 4096
D = 1024
DI = 2816
NT = NTOK // 128
TT = 512
NG = NTOK // TT
DCH = D // 128
ICH = DI // 128
KQW = 1024
QG = 4
NQG = NT // QG

ENGINES = ("sync", "vector", "scalar", "tensor", "gpsimd")


class Plan:
    def __init__(self):
        self.ops = {e: [] for e in ENGINES}
        self.cnt = {}

    def inc(self, sem, n):
        self.cnt[sem] = self.cnt.get(sem, 0) + n
        return self.cnt[sem]

    def val(self, sem):
        return self.cnt.get(sem, 0)

    def op(self, engine, fn, waits=(), incs=()):
        wl = [(s, v) for (s, v) in waits if v > 0]
        il = []
        for s, n in incs:
            self.inc(s, n)
            il.append((s, n))
        self.ops[engine].append((fn, wl, il))

    def dma(self, queue, out, in_, sem, waits=()):
        self.op(queue, lambda eng, o=out, i=in_: eng.dma_start(out=o, in_=i),
                waits=waits, incs=[(sem, 16)])
        return self.val(sem)

    def wait(self, engine, waits):
        self.op(engine, None, waits=waits)


def _bcast_ap(dram_handle, nfree, npart=128):
    a = dram_handle.ap()
    return AP(tensor=a.tensor, offset=a.offset, ap=[[0, npart], [1, nfree]])


def build_program(upto="full"):
    nc = bass.Bass("TRN2", target_bir_lowering=False, debug=False)

    x0 = nc.dram_tensor("x0", [NTOK, D], F32, kind="ExternalInput").ap()
    Rm = nc.dram_tensor("Rm", [D, D], F32, kind="ExternalInput").ap()
    wg_d = nc.dram_tensor("wg", [DI, D], F32, kind="ExternalInput").ap()
    wu_d = nc.dram_tensor("wu", [DI, D], F32, kind="ExternalInput").ap()
    wd_d = nc.dram_tensor("wd", [D, DI], F32, kind="ExternalInput").ap()
    clip_x = nc.dram_tensor("clip_x", [1], F32, kind="ExternalInput")
    clip_d = nc.dram_tensor("clip_d", [1], F32, kind="ExternalInput")
    id32 = nc.dram_tensor("id32", [128, 128], F32, kind="ExternalInput").ap()
    id16 = nc.dram_tensor("id16", [128, 128], F16, kind="ExternalInput").ap()
    idb16 = nc.dram_tensor("idb16", [128, 128], BF16, kind="ExternalInput").ap()
    maskm = nc.dram_tensor("maskm", [128, 896], F32, kind="ExternalInput").ap()
    out = nc.dram_tensor("out", [NTOK, D], F32, kind="ExternalOutput").ap()

    x1b = nc.dram_tensor("x1b", [NTOK, D], F32)
    x2b = nc.dram_tensor("x2b", [NTOK, D], F32)
    hbuf = nc.dram_tensor("hbuf", [ICH, NG, 128, TT], F32)
    wgsp = nc.dram_tensor("wgsp", [128, DCH * DI], BF16)
    wusp = nc.dram_tensor("wusp", [128, DCH * DI], BF16)
    wdsp = nc.dram_tensor("wdsp", [128, ICH * D], BF16)
    encTh_d = nc.dram_tensor("encTh_d", [NTOK, D], F16)
    encTl_d = nc.dram_tensor("encTl_d", [NTOK, D], F16)
    encFh_d = nc.dram_tensor("encFh_d", [D, NTOK], F16)
    encFl_d = nc.dram_tensor("encFl_d", [D, NTOK], F16)
    qth_d = nc.dram_tensor("qth_d", [128, NQG * DCH * TT], F16)
    qtl_d = nc.dram_tensor("qtl_d", [128, NQG * DCH * TT], F16)
    cc_in = [nc.dram_tensor(f"cc_in{i}", [1], F32) for i in range(4)]
    cc_out = [nc.dram_tensor(f"cc_out{i}", [NCORES], F32) for i in range(4)]

    p = Plan()
    es = ExitStack()
    with es:
        sbt = lambda name, shape, dt: es.enter_context(nc.sbuf_tensor(name, shape, dt))

        c_id32 = sbt("c_id32", [128, 128], F32)
        c_id16 = sbt("c_id16", [128, 128], F16)
        c_idb16 = sbt("c_idb16", [128, 128], BF16)
        c_ones16 = sbt("c_ones16", [128, 128], F16)
        c_mask = sbt("c_mask", [128, 896], F32)
        sig_x = sbt("sig_x", [128, 1], F32)
        sig_d = sbt("sig_d", [128, 1], F32)
        c127 = sbt("c127", [128, 1], F32)
        r127 = sbt("r127", [128, 1], F32)
        eps_c = sbt("eps_c", [128, 1], F32)
        c256 = sbt("c256", [128, 1], F32)
        sWg = sbt("sWg", [128, ICH], F32)
        sWu = sbt("sWu", [128, ICH], F32)
        sWd = sbt("sWd", [128, DCH], F32)
        ag = sbt("ag", [128, ICH], F32)
        au = sbt("au", [128, ICH], F32)
        adh = sbt("adh", [128, DCH], F32)
        m_col = sbt("m_col", [128, NT], F32)
        nm_col = sbt("nm_col", [128, NT], F32)
        rs_col = sbt("rs_col", [128, NT], F32)
        krx_col = sbt("krx_col", [128, NT], F32)
        amax_run = sbt("amax_run", [128, 1], F32)
        t0c = sbt("t0c", [128, NT], F32)
        t1c = sbt("t1c", [128, NT], F32)
        habs_run = sbt("habs_run", [128, 1], F32)
        habs_tmp = sbt("habs_tmp", [128, 1], F32)
        ccg = sbt("ccg", [128, NCORES], F32)
        sxv = sbt("sxv", [128, 1], F32)
        rxv = sbt("rxv", [128, 1], F32)
        shv = sbt("shv", [128, 1], F32)
        rhv = sbt("rhv", [128, 1], F32)
        t0 = sbt("t0", [128, 1], F32)
        t1 = sbt("t1", [128, 1], F32)
        t3 = sbt("t3", [1, 1], F32)
        lsum_t = sbt("lsum_t", [128, 1], F32)
        lrun = sbt("lrun", [128, NT], F32)
        mrun = sbt("mrun", [128, NT], F32)
        mt = sbt("mt", [128, 1], F32)
        mnew = sbt("mnew", [128, 1], F32)
        nmnew = sbt("nmnew", [128, 1], F32)
        fscb = [sbt(f"fsc{i}", [128, 1], F32) for i in range(2)]
        linv = sbt("linv", [128, 1], F32)

        SLOT = 49152
        arenaA = sbt("arenaA", [128, 2 * SLOT], U8)
        aA = nc.lookup_mloc(arenaA).addr

        def alias(name, shape, dt, byte_off, base=None):
            h = nc.virtual_sbuf_tensor(name, shape, dt)
            nc._tensor(h.name, list(h.shape), h.dtype, type="SB")
            mloc = nc.lookup_mloc(h)
            mloc.allocated = True
            mloc.addr = (aA if base is None else base) + byte_off
            return h

        WgT = alias("WgT", [128, DCH * DI], BF16, 0)
        wtileb = [alias(f"wtile{i}", [128, DI], F32, i * 12288) for i in range(2)]
        wi32b = [alias(f"wi32{i}", [128, DI], I32, 24576 + i * 12288) for i in range(2)]
        wcfb = [alias(f"wcf{i}", [128, DI], F32, 49152 + i * 12288) for i in range(2)]
        wq16b = [alias(f"wq16{i}", [128, DI], BF16, 73728 + i * 8192) for i in range(2)]
        evb = alias("evb", [128, 1024], BF16, 90112)
        WuT = alias("WuT", [128, DCH * DI], BF16, SLOT)
        WdT = alias("WdT", [128, ICH * D], BF16, 0)
        aFhb = [alias(f"aFh{i}", [128, DCH * 512], F16, i * 8192)
                for i in range(2)]
        aFlb = [alias(f"aFl{i}", [128, DCH * 512], F16, 16384 + i * 8192)
                for i in range(2)]
        aThb = [alias(f"aTh{i}", [128, 4 * D], F16, 32768 + i * 8192)
                for i in range(2)]
        aTlb = [alias(f"aTl{i}", [128, 4 * D], F16, 49152 + i * 8192)
                for i in range(2)]
        aQh = alias("aQh", [128, DCH * (QG * 128)], F16, 65536)
        aQl = alias("aQl", [128, DCH * (QG * 128)], F16, 73728)
        aO = alias("aO", [128, QG * D], F32, 81920)
        aRh = alias("aRh", [128, DCH * D], F16, 0)
        aRl = alias("aRl", [128, DCH * D], F16, 16384)
        qhb = [alias(f"qh{i}", [128, ICH * TT], BF16, SLOT + i * 22528)
               for i in range(2)]

        arenaB = sbt("arenaB", [128, 104448], U8)
        aB = nc.lookup_mloc(arenaB).addr
        off = [0]

        def aliasB(name, shape, dt, at=None):
            if at is not None:
                off[0] = at
            nbytes = int(np.prod(shape[1:])) * mybir.dt.size(dt)
            h = alias(name, shape, dt, off[0], base=aB)
            off[0] += (nbytes + 31) & ~31
            return h

        xa = [aliasB(f"xa{i}", [128, D], F32) for i in range(4)]
        vq = aliasB("vq", [128, D], F32)
        iq = alias("iq", [128, D], I32, 94208)
        qxtok = [aliasB(f"qxtok{i}", [128, D], BF16) for i in range(4)]
        qxTb = [aliasB(f"qxT{i}", [128, DCH * TT], BF16) for i in range(2)]
        silu_t = aliasB("silu_t", [128, TT], F32)
        h_t = [aliasB(f"h_t{i}", [128, TT], F32) for i in range(3)]
        sqs_at = off[0]
        sqs = aliasB("sqs", [128, D], F32)
        cb = aliasB("cb", [128, D], F32, at=sqs_at)
        down_at = off[0]
        hin = [aliasB(f"hin{i}", [128, TT], F32) for i in range(6)]
        hin.append(alias("hin6", [128, TT], F32, 0, base=aB))
        hin.append(alias("hin7", [128, TT], F32, 2048, base=aB))
        hvw = [aliasB(f"hvw{i}", [128, 2 * TT], F32) for i in range(2)]
        hvb = [hvw[i][:, 0:TT] for i in range(2)]
        hvb2 = [hvw[i][:, TT:2 * TT] for i in range(2)]
        hi32one = aliasB("hi32", [128, 2 * TT], I32)
        hi32b = [hi32one, hi32one]
        hcf1 = aliasB("hcf", [128, 2 * TT], F32)
        hcfb = [hcf1, hcf1]
        dsc = aliasB("dsc", [128, TT], F32)
        xout = aliasB("xout", [128, 4 * D], F32)
        ffn_end = off[0]
        wtile = aliasB("wtile", [128, DI], F32, at=down_at)
        wi32 = aliasB("wi32", [128, DI], I32)
        wcf = aliasB("wcf", [128, DI], F32)
        wq16 = aliasB("wq16", [128, DI], BF16)
        assert off[0] <= 104448, off[0]
        yab = [aliasB(f"ya{i}", [128, D], F32, at=0 if i == 0 else None)
               for i in range(2)]
        ya = yab[0]
        yhb = [aliasB(f"yh{i}", [128, D], F16) for i in range(2)]
        ylb = [aliasB(f"yl{i}", [128, D], F16) for i in range(2)]
        stg_h = aliasB("stg_h", [128, DCH * TT], F16)
        stg_l = aliasB("stg_l", [128, DCH * TT], F16)
        Pb = aliasB("Pb", [128, TT], F32)
        PTh = aliasB("PTh", [128, TT], F16)
        PTl = aliasB("PTl", [128, TT], F16)
        x1t = aliasB("x1t", [128, D], F32)
        x2t = aliasB("x2t", [128, D], F32)
        qsq = aliasB("qsq", [128, TT], F16)
        bnd = aliasB("bnd", [128, TT], F32)
        qevhb = [aliasB(f"qevh{i}", [128, TT], F16) for i in range(2)]
        qevlb = [aliasB(f"qevl{i}", [128, TT], F16) for i in range(2)]
        sq2 = aliasB("sq2", [128, D], F32)
        assert off[0] <= 104448, off[0]

        pA = es.enter_context(nc.psum_tensor("pA", [128, 512], F32))
        pB = es.enter_context(nc.psum_tensor("pB", [128, 512], F32))
        pC = es.enter_context(nc.psum_tensor("pC", [128, 512], F32))
        pD = es.enter_context(nc.psum_tensor("pD", [128, 512], F32))
        pO = es.enter_context(nc.psum_tensor("pO", [128, 1024], F32))
        pFb = es.enter_context(nc.psum_tensor("pFb", [128, 1024], BF16))
        pFh = es.enter_context(nc.psum_tensor("pFh", [128, 1024], F16))

        sem_names = (["dx", "dw", "dw0", "dw1", "wsp", "dg", "ds", "dv", "dv0",
                      "dv1", "dr", "drg", "v", "s", "t", "g", "cc", "o"]
                     + ["eA", "eB", "fA", "fB", "qs0", "qs1"]
                     + [f"xa{i}" for i in range(4)]
                     + [f"hs{i}" for i in range(4)]
                     + [f"hi{i}" for i in range(8)])
        sems = {n: es.enter_context(nc.semaphore(n)) for n in sem_names}

        def TS(eng, outap, inap, s1, s2, op0, op1=None, waits=(), incs=(), accum=None):
            def fn(e, o=outap, i=inap, a=s1, b=s2, x=op0, y=op1, ac=accum):
                kw = {"accum_out": ac} if ac is not None else {}
                if y is None:
                    return e.tensor_scalar(o, i, a, None, x, **kw)
                return e.tensor_scalar(o, i, a, b, x, y, **kw)
            p.op(eng, fn, waits=waits, incs=incs)

        def TT_(eng, outap, a, b, aop, waits=(), incs=()):
            p.op(eng, lambda e, o=outap, x=a, y=b, op=aop: e.tensor_tensor(o, x, y, op),
                 waits=waits, incs=incs)

        def COPY(eng, outap, inap, waits=(), incs=()):
            p.op(eng, lambda e, o=outap, i=inap: e.tensor_copy(o, i),
                 waits=waits, incs=incs)

        def RED(eng, outap, inap, aop, absval=False, waits=(), incs=()):
            p.op(eng, lambda e, o=outap, i=inap, op=aop, a=absval:
                 e.tensor_reduce(o, i, AX, op, apply_absolute_value=a),
                 waits=waits, incs=incs)

        def RECIP(eng, outap, inap, waits=(), incs=()):
            p.op(eng, lambda e, o=outap, i=inap: e.reciprocal(o, i),
                 waits=waits, incs=incs)

        def ACTI(outap, inap, func, bias=0.0, scale=1.0, accum=None, waits=(), incs=()):
            def fn(e, o=outap, i=inap, f=func, b=bias, s=scale, a=accum):
                kw = {"accum_out": a} if a is not None else {}
                return e.activation(o, i, f, bias=b, scale=s, **kw)
            p.op("scalar", fn, waits=waits, incs=incs)

        def STT(eng, outap, in0, scalar, in1, op0, op1, waits=(), incs=()):
            p.op(eng, lambda e, o=outap, a=in0, s=scalar, b=in1, x=op0, y=op1:
                 e.scalar_tensor_tensor(o, a, s, b, x, y), waits=waits, incs=incs)

        def MM(outap, lhsT, rhs, start, stop, waits=(), incs=(), tp=False):
            p.op("tensor", lambda e, o=outap, l=lhsT, r=rhs, st=start, sp=stop, t=tp:
                 e.matmul(o, l, r, start=st, stop=sp, is_transpose=t),
                 waits=waits, incs=incs)

        def TRG(outap, inap, ident, start, stop, waits=(), incs=()):
            MM(outap, inap, ident, start, stop, waits=waits, incs=incs, tp=True)

        def MEMSET(eng, ap_, const, waits=(), incs=()):
            p.op(eng, lambda e, a=ap_, c=const: e.memset(a, c),
                 waits=waits, incs=incs)

        p.dma("sync", c_id32[:], id32, "dx")
        p.dma("sync", c_id16[:], id16, "dx")
        p.dma("sync", c_idb16[:], idb16, "dx")
        p.dma("sync", c_mask[:], maskm, "dx")
        p.dma("sync", sig_x[:], _bcast_ap(clip_x, 1), "dx")
        p.dma("sync", sig_d[:], _bcast_ap(clip_d, 1), "dx")
        init_dx = p.val("dx")
        MEMSET("vector", c_ones16[:], 1.0, incs=[("v", 1)])
        MEMSET("vector", c127[:], 127.0, incs=[("v", 1)])
        MEMSET("vector", eps_c[:], 1e-5, incs=[("v", 1)])
        MEMSET("vector", c256[:], 256.0, incs=[("v", 1)])
        nc.const_aps.register(F32, 256.0, c256[:])
        RECIP("vector", r127[:], c127[:], waits=[("v", p.val("v"))], incs=[("v", 1)])
        ACTI(sig_x[:], sig_x[:], ACT.Sigmoid, waits=[("dx", init_dx)], incs=[("s", 1)])
        ACTI(sig_d[:], sig_d[:], ACT.Sigmoid, incs=[("s", 1)])
        init_s = p.val("s")
        p.wait("tensor", [("dx", init_dx)])
        p.wait("vector", [("dx", init_dx)])
        p.wait("gpsimd", [("dx", init_dx)])

        prep_free = {}

        def prep_weight(src_ap, ntiles, freew, sW_t, spill, base):
            nch = freew // 128
            sp3 = spill.ap().rearrange("p (c n) -> p c n", c=nch)
            ev3 = evb.ap().rearrange("p (c n) -> p c n", c=8)
            tiles = []
            for it in range(ntiles):
                def emit(it=it):
                    gt = base + it
                    pp = gt % 2
                    wtile = wtileb[pp]
                    wi32 = wi32b[pp]
                    wcf = wcfb[pp]
                    wq16 = wq16b[pp]
                    sw_ = f"dw{pp}"
                    ld = p.dma("sync", wtile[:, 0:freew],
                               src_ap[it * 128:(it + 1) * 128, :], sw_,
                               waits=[("v", prep_free.get(gt - 2, 0))])
                    RED("vector", t0[:], wtile[:, 0:freew], ALU.max,
                        absval=True,
                        waits=[(sw_, ld), ("v", p.val("v"))], incs=[("v", 1)])
                    TT_("vector", sW_t[:, it:it + 1], t0[:], r127[:], ALU.mult,
                        waits=[("v", p.val("v"))], incs=[("v", 1)])
                    RECIP("vector", t1[:], sW_t[:, it:it + 1],
                          waits=[("v", p.val("v"))], incs=[("v", 1)])
                    TS("vector", wtile[:, 0:freew], wtile[:, 0:freew], t1[:],
                       256.0, ALU.mult, ALU.add,
                       waits=[("v", p.val("v"))], incs=[("v", 1)])
                    p.op("scalar",
                         lambda e, o=wi32[:, 0:freew], i=wtile[:, 0:freew]:
                         e.copy(o, i),
                         waits=[("v", p.val("v")), ("s", p.val("s"))],
                         incs=[("s", 1)])
                    STT("vector", wcf[:, 0:freew], wi32[:, 0:freew], 0.5,
                        wtile[:, 0:freew], ALU.add, ALU.is_le,
                        waits=[("s", p.val("s")), ("v", p.val("v"))],
                        incs=[("v", 1)])
                    STT("vector", wq16[:, 0:freew], wi32[:, 0:freew], -256.0,
                        wcf[:, 0:freew], ALU.add, ALU.add,
                        waits=[("v", p.val("v"))], incs=[("v", 1)])
                    prep_free[gt] = p.val("v")
                    c = 0
                    while c < nch:
                        nb = min(8, nch - c)
                        t_w = p.val("t")
                        for j in range(nb):
                            TRG(pFb[:, j * 128:(j + 1) * 128],
                                wq16[:, (c + j) * 128:(c + j + 1) * 128],
                                c_idb16[:],
                                start=(j == 0), stop=(j == nb - 1),
                                waits=([("v", p.val("v")), ("t", t_w)]
                                       if j == 0 else ()),
                                incs=([("t", 1)] if j == nb - 1 else ()))
                        COPY("vector", ev3[:, 0:nb, :], pFb[:, 0:nb * 128],
                             waits=[("t", p.val("t")), ("wsp", p.val("wsp"))],
                             incs=[("v", 1)])
                        p.dma("gpsimd",
                              sp3[:, c:c + nb, it * 128:(it + 1) * 128],
                              ev3[:, 0:nb, :], "wsp", waits=[("v", p.val("v"))])
                        c += nb
                tiles.append(emit)
            return tiles

        prep_tiles = (prep_weight(wg_d, ICH, D, sWg, wgsp, 0)
                      + prep_weight(wu_d, ICH, D, sWu, wusp, ICH)
                      + prep_weight(wd_d, DCH, DI, sWd, wdsp, 2 * ICH))

        def preduce_max(in_col):
            TRG(pC[0:1, 0:128], in_col, c_id32[:], start=True, stop=True,
                waits=[("t", p.val("t")), ("v", p.val("v")), ("s", p.val("s"))],
                incs=[("t", 1)])
            RED("vector", t3[:], pC[0:1, 0:128], ALU.max,
                waits=[("t", p.val("t"))], incs=[("v", 1)])

        def collective(idx, out_col):
            st = p.dma("gpsimd", cc_in[idx].ap(), t3[:], "dg",
                       waits=[("v", p.val("v"))])

            def fn(e, i=idx):
                return e.collective_compute(
                    "AllGather", ALU.bypass,
                    replica_groups=[list(range(NCORES))],
                    ins=[cc_in[i].ap().opt()],
                    outs=[cc_out[i].ap().opt()],
                )
            p.op("gpsimd", fn, waits=[("dg", st)], incs=[("cc", 1)])
            ld = p.dma("gpsimd", ccg[:], _bcast_ap(cc_out[idx], NCORES), "dg",
                       waits=[("cc", p.val("cc"))])
            RED("vector", out_col, ccg[:], ALU.max, waits=[("dg", ld)],
                incs=[("v", 1)])

        def ffn_call(x_in, x_out, cidx, wload_gu, pre_waits=(), prep=None):
            xin_t = x_in.rearrange("(n q) d -> n q d", q=128)
            xout_t = x_out.rearrange("(n q) d -> n q d", q=128)
            xin_g = x_in.rearrange("(n q) d -> q n d", q=128)

            pa_done = {}
            ppi = 0
            for nt in range(NT):
                buf = xa[nt % 4]
                sx_ = f"xa{nt % 4}"
                pv, ps_ = pa_done.get(nt - 4, (0, 0))
                w = [("v", pv), ("s", ps_)]
                if nt < 4:
                    w += list(pre_waits)
                ld = p.dma("sync", buf[:], xin_t[nt], sx_, waits=w)
                RED("vector", t0c[:, nt:nt + 1], buf[:], ALU.max,
                    waits=[(sx_, ld), ("v", p.val("v"))], incs=[("v", 1)])
                RED("vector", t1c[:, nt:nt + 1], buf[:], ALU.min, incs=[("v", 1)])
                RED("vector", m_col[:, nt:nt + 1], buf[:], ALU.add, incs=[("v", 1)])
                TS("vector", m_col[:, nt:nt + 1], m_col[:, nt:nt + 1], 1.0 / D,
                   None, ALU.mult, waits=[("v", p.val("v"))], incs=[("v", 1)])
                TS("vector", nm_col[:, nt:nt + 1], m_col[:, nt:nt + 1], -1.0,
                   None, ALU.mult, waits=[("v", p.val("v"))], incs=[("v", 1)])
                ACTI(sqs[:], buf[:], ACT.Square, bias=nm_col[:, nt:nt + 1],
                     scale=1.0, accum=rs_col[:, nt:nt + 1],
                     waits=[("v", p.val("v")), ("s", p.val("s"))],
                     incs=[("s", 1)])
                ACTI(rs_col[:, nt:nt + 1], rs_col[:, nt:nt + 1], ACT.Sqrt,
                     bias=eps_c[:], scale=1.0 / D, waits=[("s", p.val("s"))],
                     incs=[("s", 1)])
                pa_done[nt] = (p.val("v"), p.val("s"))
                if prep:
                    tgt = (nt + 1) * len(prep) // NT
                    while ppi < tgt:
                        prep[ppi]()
                        ppi += 1
            if prep:
                p.dma("sync", WgT[:], wgsp.ap(), "dw",
                      waits=[("wsp", p.val("wsp"))])
                p.dma("sync", WuT[:], wusp.ap(), "dw")
                wload_gu = p.val("dw")
            RECIP("vector", rs_col[:], rs_col[:],
                  waits=[("s", p.val("s")), ("v", p.val("v"))], incs=[("v", 1)])
            TT_("vector", t0c[:], t0c[:], m_col[:], ALU.subtract,
                waits=[("v", p.val("v"))], incs=[("v", 1)])
            TT_("vector", t1c[:], m_col[:], t1c[:], ALU.subtract,
                incs=[("v", 1)])
            TT_("vector", t0c[:], t0c[:], t1c[:], ALU.max,
                waits=[("v", p.val("v"))], incs=[("v", 1)])
            TT_("vector", t0c[:], t0c[:], rs_col[:], ALU.mult,
                waits=[("v", p.val("v"))], incs=[("v", 1)])
            RED("vector", amax_run[:], t0c[:], ALU.max,
                waits=[("v", p.val("v"))], incs=[("v", 1)])
            preduce_max(amax_run[:])
            collective(2 * cidx, t0[:])
            TT_("vector", t0[:], t0[:], sig_x[:], ALU.mult,
                waits=[("v", p.val("v")), ("s", init_s)], incs=[("v", 1)])
            TT_("vector", sxv[:], t0[:], r127[:], ALU.mult,
                waits=[("v", p.val("v"))], incs=[("v", 1)])
            RECIP("vector", rxv[:], sxv[:], waits=[("v", p.val("v"))],
                  incs=[("v", 1)])
            TS("vector", krx_col[:], rs_col[:], rxv[:], None, ALU.mult,
               waits=[("v", p.val("v"))], incs=[("v", 1)])
            TS("vector", ag[:], sWg[:], sxv[:], None, ALU.mult,
               waits=[("v", p.val("v"))], incs=[("v", 1)])
            TS("vector", au[:], sWu[:], sxv[:], None, ALU.mult, incs=[("v", 1)])
            scales_v = p.val("v")

            MEMSET("vector", habs_run[:], 0.0, waits=[("v", p.val("v"))],
                   incs=[("v", 1)])
            wgT3 = WgT.ap().rearrange("p (c n) -> p c n", c=DCH)
            wuT3 = WuT.ap().rearrange("p (c n) -> p c n", c=DCH)
            pFb3 = pFb.ap().rearrange("p (c n) -> p c n", c=DCH)
            base_s = p.val("s")
            sil_done = {}
            stt_done = {}
            mm_done_g = {}
            xq_done = {}
            qdone_t = {}
            last_sp = {i: p.val(f"hs{i}") for i in range(3)}

            def quant_vec(g):
                for j in range(4):
                    nt = g * 4 + j
                    buf = xa[j]
                    sx_ = f"xa{j}"
                    ld = p.dma("sync", buf[:], xin_t[nt], sx_,
                               waits=[("v", scales_v if nt < 4
                                       else xq_done.get(nt - 4, 0))])
                    TS("vector", vq[:], buf[:], m_col[:, nt:nt + 1],
                       krx_col[:, nt:nt + 1], ALU.subtract, ALU.mult,
                       waits=[(sx_, ld), ("v", scales_v)], incs=[("v", 1)])
                    TS("vector", vq[:], vq[:], 127.0, -128.0, ALU.min, ALU.max,
                       waits=[("v", p.val("v"))], incs=[("v", 1)])
                    TS("vector", vq[:], vq[:], 256.0, None, ALU.add,
                       waits=[("v", p.val("v"))], incs=[("v", 1)])
                    p.op("scalar", lambda e, o=iq[:], i=vq[:]: e.copy(o, i),
                         waits=[("v", p.val("v")), ("s", p.val("s"))],
                         incs=[("s", 1)])
                    STT("vector", cb[:], iq[:], 0.5, vq[:], ALU.add, ALU.is_le,
                        waits=[("s", p.val("s"))], incs=[("v", 1)])
                    STT("vector", qxtok[j][:], iq[:], -256.0, cb[:],
                        ALU.add, ALU.add,
                        waits=[("v", p.val("v")),
                               ("t", qdone_t.get(g - 1, 0))], incs=[("v", 1)])
                    xq_done[nt] = p.val("v")

            def transp(g):
                qxT = qxTb[g % 2]
                qxT3 = qxT.ap().rearrange("p (c n) -> p c n", c=DCH)
                for j in range(4):
                    nt = g * 4 + j
                    t_w = p.val("t")
                    for k in range(DCH):
                        TRG(pFb[:, k * 128:(k + 1) * 128],
                            qxtok[j][:, k * 128:(k + 1) * 128], c_idb16[:],
                            start=(k == 0), stop=(k == DCH - 1),
                            waits=([("v", max(xq_done[nt], p.val("v"))),
                                    ("t", max(t_w, mm_done_g.get(g - 2, 0)))]
                                   if k == 0 else ()),
                            incs=([("t", 1)] if k == DCH - 1 else ()))
                    COPY("vector", qxT3[:, :, j * 128:(j + 1) * 128], pFb3,
                         waits=[("t", p.val("t"))], incs=[("v", 1)])
                qdone_t[g] = p.val("t")

            def iloop(g):
                qxT = qxTb[g % 2]
                qx_ready = p.val("v")
                base_v = qx_ready
                for i in range(ICH):
                    gi = g * ICH + i
                    pg = pA if gi % 2 == 0 else pC
                    pu = pB if gi % 2 == 0 else pD
                    for k in range(DCH):
                        MM(pg[:], wgT3[:, k, i * 128:(i + 1) * 128],
                           qxT[:, k * TT:(k + 1) * TT],
                           start=(k == 0), stop=(k == DCH - 1),
                           waits=([("v", qx_ready), ("dw", wload_gu),
                                   ("s", sil_done.get(gi - 2, base_s))]
                                  if k == 0 else ()),
                           incs=([("t", 1)] if k == DCH - 1 else ()))
                    ga_t = p.val("t")
                    for k in range(DCH):
                        MM(pu[:], wuT3[:, k, i * 128:(i + 1) * 128],
                           qxT[:, k * TT:(k + 1) * TT],
                           start=(k == 0), stop=(k == DCH - 1),
                           waits=([("v", stt_done.get(gi - 2, base_v))]
                                  if k == 0 else ()),
                           incs=([("t", 1)] if k == DCH - 1 else ()))
                    up_t = p.val("t")
                    ACTI(silu_t[:], pg[:], ACT.Silu, bias=0.0,
                         scale=ag[:, i:i + 1],
                         waits=[("t", ga_t), ("v", p.val("v"))], incs=[("s", 1)])
                    sil_done[gi] = p.val("s")
                    hb = h_t[gi % 3]
                    STT("vector", hb[:], pu[:], au[:, i:i + 1], silu_t[:],
                        ALU.mult, ALU.mult,
                        waits=[("t", up_t), ("s", p.val("s")),
                               (f"hs{gi % 3}", last_sp[gi % 3])],
                        incs=[("v", 1)])
                    stt_done[gi] = p.val("v")
                    hv_v = p.val("v")
                    RED("vector", habs_tmp[:], hb[:], ALU.max, absval=True,
                        waits=[("v", hv_v)], incs=[("v", 1)])
                    TT_("vector", habs_run[:], habs_run[:], habs_tmp[:], ALU.max,
                        waits=[("v", p.val("v"))], incs=[("v", 1)])
                    p.dma("gpsimd", hbuf.ap()[i, g], hb[:], f"hs{gi % 3}",
                          waits=[("v", hv_v)])
                    last_sp[gi % 3] = p.val(f"hs{gi % 3}")
                mm_done_g[g] = p.val("t")

            quant_vec(0)
            transp(0)
            for g in range(NG):
                if g + 1 < NG:
                    quant_vec(g + 1)
                iloop(g)
                if g + 1 < NG:
                    transp(g + 1)

            preduce_max(habs_run[:])
            collective(2 * cidx + 1, t0[:])
            TT_("vector", t0[:], t0[:], sig_d[:], ALU.mult,
                waits=[("v", p.val("v"))], incs=[("v", 1)])
            TT_("vector", shv[:], t0[:], r127[:], ALU.mult,
                waits=[("v", p.val("v"))], incs=[("v", 1)])
            RECIP("vector", rhv[:], shv[:], waits=[("v", p.val("v"))],
                  incs=[("v", 1)])
            TS("vector", adh[:], sWd[:], shv[:], 0.5, ALU.mult, ALU.mult,
               waits=[("v", p.val("v"))], incs=[("v", 1)])
            hscales_v = p.val("v")

            p.dma("sync", WdT[:], wdsp.ap(), "dw", waits=[("t", p.val("t"))])
            wload_d = p.val("dw")

            wdT3 = WdT.ap().rearrange("p (c n) -> p c n", c=ICH)
            xo4 = xout.ap().rearrange("p (j d) -> p j d", j=4)
            p.wait("sync", [(f"hs{i}", p.val(f"hs{i}")) for i in range(4)])
            dsc_done = {}
            dmm_done = {}
            hin_free = {}
            hv_free = {}
            qh_ready = {}

            def quant_group(g):
                qh = qhb[g % 2]
                qh3 = qh.ap().rearrange("p (c n) -> p c n", c=ICH)
                mm_prev = dmm_done.get(g - 2, p.val("t") if g == 0 else 0)
                g0v = p.val("v") if g == 0 else 0
                for ci in range(ICH // 2):
                    c = 2 * ci
                    gci = g * (ICH // 2) + ci
                    pp = gci % 2
                    p8 = gci % 4
                    buf = hin[2 * p8]
                    buf2 = hin[2 * p8 + 1]
                    hv = hvb[pp]
                    hv2 = hvb2[pp]
                    hi32 = hi32b[pp]
                    hcf = hcfb[pp]
                    sh_ = f"hi{2 * p8}"
                    sh2_ = f"hi{2 * p8 + 1}"
                    ld1 = p.dma("sync", buf[:], hbuf.ap()[c, g], sh_,
                                waits=[("s", hin_free.get((gci - 4, 0), 0)),
                                       ("v", g0v)])
                    ld2 = p.dma("sync", buf2[:], hbuf.ap()[c + 1, g], sh2_,
                                waits=[("s", hin_free.get((gci - 4, 1), 0)),
                                       ("v", g0v)])
                    ACTI(hv, buf[:], ACT.Copy, bias=256.0, scale=rhv[:],
                         waits=[(sh_, ld1),
                                ("v", hscales_v if g == 0 and ci == 0
                                 else hv_free.get(gci - 2, 0)),
                                ("t", mm_prev)], incs=[("s", 1)])
                    hin_free[(gci, 0)] = p.val("s")
                    ACTI(hv2, buf2[:], ACT.Copy, bias=256.0, scale=rhv[:],
                         waits=[(sh2_, ld2)], incs=[("s", 1)])
                    hin_free[(gci, 1)] = p.val("s")
                    TS("vector", hvw[pp][:], hvw[pp][:], 383.0, 128.0,
                       ALU.min, ALU.max,
                       waits=[("s", p.val("s")), ("v", p.val("v"))],
                       incs=[("v", 1)])
                    p.op("scalar", lambda e, o=hi32[:], i=hvw[pp][:]:
                         e.copy(o, i),
                         waits=[("v", p.val("v"))], incs=[("s", 1)])
                    STT("vector", hcf[:], hi32[:], 0.5, hvw[pp][:], ALU.add,
                        ALU.is_le, waits=[("s", p.val("s")),
                                          ("v", p.val("v"))], incs=[("v", 1)])
                    hv_free[gci] = p.val("v")
                    STT("vector", qh3[:, c:c + 2, :], hi32[:], -256.0, hcf[:],
                        ALU.add, ALU.add,
                        waits=[("v", p.val("v"))], incs=[("v", 1)])
                qh_ready[g] = p.val("v")

            def down_group(g):
                qh = qhb[g % 2]
                qh3 = qh.ap().rearrange("p (c n) -> p c n", c=ICH)
                for j in range(4):
                    xld = p.dma("gpsimd", xo4[:, j, :],
                                xin_g[:, g * 4 + j, :], "drg",
                                waits=([("o", p.val("o"))] if j == 0 else ()))
                tr_done = {}
                for hc in range(DCH):
                    pd_ = pA if hc % 2 == 0 else pC
                    for c in range(ICH):
                        MM(pd_[:], wdT3[:, c, hc * 128:(hc + 1) * 128],
                           qh3[:, c, :], start=(c == 0), stop=(c == ICH - 1),
                           waits=([("v", dsc_done.get((g, hc - 2),
                                        qh_ready[g])),
                                   ("dw", wload_d)] if c == 0 else ()),
                           incs=([("t", 1)] if c == ICH - 1 else ()))
                    dmm = p.val("t")
                    if hc > 0:
                        emit_epi(g, hc - 1, tr_done)
                    tr_done[hc] = dmm
                emit_epi(g, DCH - 1, tr_done)
                dmm_done[g] = p.val("t")
                res_v = p.val("v")
                for j in range(4):
                    p.dma("gpsimd", xout_t[g * 4 + j],
                          xout[:, j * D:(j + 1) * D], "o",
                          waits=[("v", res_v)] if j == 0 else ())

            def emit_epi(g, hc, tr_done):
                pd_ = pA if hc % 2 == 0 else pC
                ptr_ = pB if hc % 2 == 0 else pD
                TS("vector", dsc[:], pd_[:], adh[:, hc:hc + 1], None,
                   ALU.mult, waits=[("t", tr_done[hc]), ("v", p.val("v"))],
                   incs=[("v", 1)])
                dsc_done[(g, hc)] = p.val("v")
                t_w2 = p.val("t")
                for j in range(4):
                    TRG(ptr_[:, j * 128:(j + 1) * 128],
                        dsc[:, j * 128:(j + 1) * 128], c_id32[:],
                        start=(j == 0), stop=(j == 3),
                        waits=([("v", dsc_done[(g, hc)]), ("t", t_w2)]
                               if j == 0 else ()),
                        incs=([("t", 1)] if j == 3 else ()))
                tr_t = p.val("t")
                for j in range(4):
                    TT_("vector",
                        xout[:, j * D + hc * 128: j * D + (hc + 1) * 128],
                        ptr_[:, j * 128:(j + 1) * 128],
                        xout[:, j * D + hc * 128: j * D + (hc + 1) * 128],
                        ALU.add,
                        waits=([("t", tr_t), ("drg", p.val("drg"))]
                               if j == 0 else ()),
                        incs=([("v", 1)] if j == 3 else ()))

            quant_group(0)
            for g in range(NG):
                if g + 1 < NG:
                    quant_group(g + 1)
                down_group(g)
        def attention():
            x1_t = x1b.ap().rearrange("(n q) d -> n q d", q=128)
            x2_t = x2b.ap().rearrange("(n q) d -> n q d", q=128)
            eTh4 = encTh_d.ap().rearrange("(n q) d -> n q d", q=128)
            eTl4 = encTl_d.ap().rearrange("(n q) d -> n q d", q=128)
            eThq = encTh_d.ap().rearrange("(c q) d -> q c d", q=128)
            eTlq = encTl_d.ap().rearrange("(c q) d -> q c d", q=128)
            eFh3 = encFh_d.ap().rearrange("(c q) n -> q c n", q=128)
            eFl3 = encFl_d.ap().rearrange("(c q) n -> q c n", q=128)
            qth2 = qth_d.ap()
            qtl2 = qtl_d.ap()
            stgh3 = stg_h.ap().rearrange("p (c n) -> p c n", c=DCH)
            stgl3 = stg_l.ap().rearrange("p (c n) -> p c n", c=DCH)
            aRh3 = aRh.ap().rearrange("p (c n) -> p c n", c=DCH)
            aRl3 = aRl.ap().rearrange("p (c n) -> p c n", c=DCH)
            pFh3 = pFh.ap().rearrange("p (c n) -> p c n", c=DCH)

            for k in range(DCH):
                ld = p.dma("sync", ya[:], Rm[k * 128:(k + 1) * 128, :], "dr",
                           waits=[("v", p.val("v")), ("t", p.val("t"))])
                COPY("vector", aRh3[:, k, :], ya[:], waits=[("dr", ld)],
                     incs=[("v", 1)])
                TT_("vector", aRl3[:, k, :], ya[:], aRh3[:, k, :], ALU.subtract,
                    waits=[("v", p.val("v"))], incs=[("v", 1)])
            r_ready = p.val("v")

            ya_free = {}
            y_free = {}
            dv_free = {}
            qev_done = {}
            qsp_done = {}
            for grp in range(NG):
                for j in range(4):
                    nt = grp * 4 + j
                    pp = nt % 2
                    ya_ = yab[pp]
                    yh = yhb[pp]
                    yl = ylb[pp]
                    sy_ = f"xa{pp}"
                    w = [("v", ya_free.get(nt - 2, 0)),
                         ("s", p.val("s") if nt < 2 else 0)]
                    if nt == 0:
                        w += [("o", p.val("o")), ("v", p.val("v"))]
                    ld = p.dma("sync", ya_[:], x1_t[nt], sy_, waits=w)
                    RED("vector", m_col[:, nt:nt + 1], ya_[:], ALU.add,
                        waits=[(sy_, ld), ("v", p.val("v"))], incs=[("v", 1)])
                    TS("vector", nm_col[:, nt:nt + 1], m_col[:, nt:nt + 1],
                       -1.0 / D, None, ALU.mult, waits=[("v", p.val("v"))],
                       incs=[("v", 1)])
                    ACTI(sq2[:], ya_[:], ACT.Square, bias=nm_col[:, nt:nt + 1],
                         scale=1.0, accum=rs_col[:, nt:nt + 1],
                         waits=[("v", p.val("v")), ("s", p.val("s"))],
                         incs=[("s", 1)])
                    ACTI(rs_col[:, nt:nt + 1], rs_col[:, nt:nt + 1], ACT.Sqrt,
                         bias=eps_c[:], scale=1.0 / D, waits=[("s", p.val("s"))],
                         incs=[("s", 1)])
                    RECIP("vector", rs_col[:, nt:nt + 1], rs_col[:, nt:nt + 1],
                          waits=[("s", p.val("s")), ("v", p.val("v"))],
                          incs=[("v", 1)])
                    TS("vector", ya_[:], ya_[:], nm_col[:, nt:nt + 1],
                       rs_col[:, nt:nt + 1], ALU.add, ALU.mult,
                       waits=[("v", p.val("v"))], incs=[("v", 1)])
                    COPY("vector", yh[:], ya_[:],
                         waits=[("v", max(p.val("v"), y_free.get(nt - 2, 0))),
                                (f"dv{pp}", dv_free.get(nt - 2, 0))],
                         incs=[("v", 1)])
                    TT_("vector", yl[:], ya_[:], yh[:], ALU.subtract,
                        waits=[("v", p.val("v"))], incs=[("v", 1)])
                    y_v = p.val("v")
                    ya_free[nt] = y_v
                    p.dma("gpsimd", eTh4[nt], yh[:], f"dv{pp}",
                          waits=[("v", y_v)])
                    p.dma("gpsimd", eTl4[nt], yl[:], f"dv{pp}")
                    dv_free[nt] = p.val(f"dv{pp}")
                    t_w = p.val("t")
                    for k in range(DCH):
                        TRG(pFh[:, k * 128:(k + 1) * 128],
                            yh[:, k * 128:(k + 1) * 128], c_id16[:],
                            start=(k == 0), stop=(k == DCH - 1),
                            waits=([("v", y_v), ("t", t_w)] if k == 0 else ()),
                            incs=([("t", 1)] if k == DCH - 1 else ()))
                    COPY("vector", stgh3[:, :, j * 128:(j + 1) * 128], pFh3,
                         waits=[("t", p.val("t")), ("dg", p.val("dg"))],
                         incs=[("v", 1)])
                    t_w = p.val("t")
                    for k in range(DCH):
                        TRG(pFh[:, k * 128:(k + 1) * 128],
                            yl[:, k * 128:(k + 1) * 128], c_id16[:],
                            start=(k == 0), stop=(k == DCH - 1),
                            waits=([("v", p.val("v")), ("t", t_w)] if k == 0
                                   else ()),
                            incs=([("t", 1)] if k == DCH - 1 else ()))
                    COPY("vector", stgl3[:, :, j * 128:(j + 1) * 128], pFh3,
                         waits=[("t", p.val("t"))], incs=[("v", 1)])
                    y_free[nt] = p.val("v")
                stg_v = p.val("v")
                p.dma("gpsimd", eFh3[:, :, grp * TT:(grp + 1) * TT], stgh3, "dg",
                      waits=[("v", stg_v)])
                p.dma("gpsimd", eFl3[:, :, grp * TT:(grp + 1) * TT], stgl3, "dg")
                for ec in range(DCH):
                    pq = pA if ec % 2 == 0 else pC
                    gec = grp * DCH + ec
                    n = 0
                    for (lh, rh) in ((aRh3, stgh3), (aRh3, stgl3), (aRl3, stgh3)):
                        for k in range(DCH):
                            n += 1
                            MM(pq[:], lh[:, k, ec * 128:(ec + 1) * 128],
                               rh[:, k, :], start=(n == 1), stop=(n == 24),
                               waits=([("v", max(stg_v, qev_done.get(
                                             gec - 2, 0))),
                                       ("s", p.val("s") if gec == 0 else 0),
                                       ("t", p.val("t") if gec == 0 else 0)]
                                      if n == 1 else ()),
                               incs=([("t", 1)] if n == 24 else ()))
                    qp_t = p.val("t")
                    qeh = qevhb[ec % 2]
                    qel = qevlb[ec % 2]
                    COPY("vector", qeh[:], pq[:],
                         waits=[("t", qp_t),
                                (f"qs{ec % 2}", qsp_done.get(gec - 2, 0)),
                                ("v", p.val("v"))], incs=[("v", 1)])
                    TT_("vector", qel[:], pq[:], qeh[:], ALU.subtract,
                        waits=[("v", p.val("v"))], incs=[("v", 1)])
                    qev_done[gec] = p.val("v")
                    q_v = p.val("v")
                    qsem = f"qs{ec % 2}"
                    qo = grp * DCH * TT + ec * TT
                    p.dma("scalar", qth2[:, qo:qo + TT],
                          qeh[:], qsem, waits=[("v", q_v)])
                    p.dma("scalar", qtl2[:, qo:qo + TT],
                          qel[:], qsem)
                    qsp_done[gec] = p.val(qsem)
            prepass_dv0 = p.val("dv0")
            prepass_dv1 = p.val("dv1")
            prepass_dg = p.val("dg")
            prepass_qs0 = p.val("qs0")
            prepass_qs1 = p.val("qs1")

            aQh3 = aQh.ap().rearrange("p (c n) -> p c n", c=DCH)
            aQl3 = aQl.ap().rearrange("p (c n) -> p c n", c=DCH)
            MEMSET("vector", lrun[:], 0.0, waits=[("v", p.val("v"))],
                   incs=[("v", 1)])
            MEMSET("vector", mrun[:], -1e30, incs=[("v", 1)])
            s_bank_free = {}
            pt_bank_free = [0]
            pb_free_t = [0]
            s_done_t = {}
            av_done_t = {}
            pt_done_v = {}

            def emit_S(pi, qt, kt, kb, qg, eld, ql, b):
                ps = (pA, pC, pD)[pi % 3]
                sfree = s_bank_free.get(pi % 3, (0, 0))
                aFh_ = aF3[0][b % 2]
                aFl_ = aF3[1][b % 2]
                n = 0
                for (lh, rh) in ((aQh3, aFh_), (aQh3, aFl_),
                                 (aQl3, aFh_)):
                    for k in range(DCH):
                        n += 1
                        MM(ps[:], lh[:, k, qt * 128:(qt + 1) * 128],
                           rh[:, k, 0:512],
                           start=(n == 1), stop=(n == 24),
                           waits=([eld, ("ds", ql),
                                   ("s", sfree[1]), ("v", sfree[0])]
                                  if n == 1 else ()),
                           incs=([("t", 1)] if n == 24 else ()))
                s_done_t[pi] = p.val("t")

            def emit_VSP(pi, qt, kt, kb, qg):
                ps = (pA, pC, pD)[pi % 3]
                Q = qg * QG + qt
                K2 = kb
                s_t = s_done_t[pi]
                if K2 == Q // 4:
                    offm = 128 * Q - 512 * K2
                    TT_("vector", ps[:], ps[:],
                        c_mask[:, 384 - offm:896 - offm], ALU.add,
                        waits=[("t", s_t), ("v", p.val("v"))], incs=[("v", 1)])
                RED("vector", mt[:], ps[:], ALU.max,
                    waits=[("t", s_t), ("v", p.val("v"))], incs=[("v", 1)])
                TT_("vector", mnew[:], mrun[:, Q:Q + 1], mt[:], ALU.max,
                    waits=[("v", p.val("v"))], incs=[("v", 1)])
                TS("vector", nmnew[:], mnew[:], -1.0, None, ALU.mult,
                   waits=[("v", p.val("v"))], incs=[("v", 1)])
                mv = p.val("v")
                ACTI(fsc_s[pi][:], mrun[:, Q:Q + 1], ACT.Exp, bias=nmnew[:],
                     scale=1.0, waits=[("v", mv), ("s", p.val("s"))],
                     incs=[("s", 1)])
                COPY("vector", mrun[:, Q:Q + 1], mnew[:],
                     waits=[("s", p.val("s"))], incs=[("v", 1)])
                ACTI(Pb[:], ps[:], ACT.Exp, bias=nmnew[:], scale=1.0,
                     accum=lsum_t[:],
                     waits=[("v", p.val("v")), ("t", pb_free_t[0])],
                     incs=[("s", 1)])
                s_bank_free[pi % 3] = (p.val("v"), p.val("s"))
                STT("vector", lrun[:, Q:Q + 1], lrun[:, Q:Q + 1], fsc_s[pi][:],
                    lsum_t[:], ALU.mult, ALU.add,
                    waits=[("s", p.val("s"))], incs=[("v", 1)])

            def emit_TRG_mm(pi):
                t_w = p.val("t")
                for j in range(4):
                    TRG(pB[:, j * 128:(j + 1) * 128],
                        Pb[:, j * 128:(j + 1) * 128], c_id32[:],
                        start=(j == 0), stop=(j == 3),
                        waits=([("s", p.val("s")), ("t", t_w),
                                ("v", pt_bank_free[0])]
                               if j == 0 else ()),
                        incs=([("t", 1)] if j == 3 else ()))
                pb_free_t[0] = p.val("t")

            def emit_PT_evac(pi):
                COPY("vector", PTh[:], pB[:],
                     waits=[("t", p.val("t")), ("v", p.val("v"))],
                     incs=[("v", 1)])
                TT_("vector", PTl[:], pB[:], PTh[:], ALU.subtract,
                    waits=[("v", p.val("v"))], incs=[("v", 1)])
                pt_v = p.val("v")
                pt_bank_free[0] = pt_v
                pt_done_v[pi] = pt_v

            def emit_AV(pi, b):
                aTh_ = aT3[0][b % 2]
                aTl_ = aT3[1][b % 2]
                n = 0
                for dh in range(2):
                    for (lh, rh) in ((PTh, aTh_), (PTh, aTl_),
                                     (PTl, aTh_)):
                        for kc in range(4):
                            n += 1
                            MM(pO[:, dh * 512:(dh + 1) * 512],
                               lh[:, kc * 128:(kc + 1) * 128],
                               rh[:, kc, dh * 512:(dh + 1) * 512],
                               start=((n % 12) == 1), stop=((n % 12) == 0),
                               waits=([("v", pt_done_v[pi]), enc_T_ld[b]]
                                      if n == 1 else ()),
                               incs=([("t", 1)] if n == 24 else ()))
                av_done_t[pi] = p.val("t")

            def emit_merge(pi, qt, kb, qg, aO_init):
                STT("vector", aO[:, qt * D:(qt + 1) * D],
                    aO[:, qt * D:(qt + 1) * D], fsc_s[pi][:], pO[:],
                    ALU.mult, ALU.add,
                    waits=[("t", av_done_t[pi]), ("v", aO_init)],
                    incs=[("v", 1)])

            fsc_s = {}
            aF3 = [[b.ap().rearrange("p (c n) -> p c n", c=DCH) for b in aFhb],
                   [b.ap().rearrange("p (c n) -> p c n", c=DCH) for b in aFlb]]
            aT3 = [[b.ap().rearrange("p (c n) -> p c n", c=4) for b in aThb],
                   [b.ap().rearrange("p (c n) -> p c n", c=4) for b in aTlb]]
            blocks = [(qg, kb) for qg in range(NQG) for kb in range(qg + 1)]
            blk_mm_done = {}
            enc_ld = {}
            qld = {}

            blk_s_done = {}
            enc_T_ld = {}

            def prefetch_F(b):
                if b >= len(blocks):
                    return
                qg, kb = blocks[b]
                pe = b % 2
                se = "fA" if pe == 0 else "fB"
                w = [("t", blk_s_done.get(b - 2, 0))]
                if b == 0:
                    w += [("t", p.val("t")), ("v", p.val("v")),
                          ("dg", prepass_dg), ("dv0", prepass_dv0),
                          ("dv1", prepass_dv1)]
                p.dma("sync", aF3[0][pe], eFh3[:, :, kb * 512:(kb + 1) * 512],
                      se, waits=w)
                enc_ld[b] = (se, p.dma("sync", aF3[1][pe],
                                       eFl3[:, :, kb * 512:(kb + 1) * 512],
                                       se))

            def prefetch_T(b):
                if b >= len(blocks):
                    return
                qg, kb = blocks[b]
                pe = b % 2
                se = "eA" if pe == 0 else "eB"
                w = [("t", blk_mm_done.get(b - 2, 0))]
                if b == 0:
                    w += [("t", p.val("t")), ("v", p.val("v")),
                          ("dg", prepass_dg), ("dv0", prepass_dv0),
                          ("dv1", prepass_dv1)]
                p.dma("sync", aT3[0][pe], eThq[:, kb * 4:(kb + 1) * 4, :], se,
                      waits=w)
                enc_T_ld[b] = (se, p.dma("sync", aT3[1][pe],
                                         eTlq[:, kb * 4:(kb + 1) * 4, :],
                                         se))

            aO_inits = {}

            def finalize(qg):
                for qt in range(QG):
                    Q = qg * QG + qt
                    RECIP("vector", linv[:], lrun[:, Q:Q + 1],
                          waits=[("v", p.val("v"))], incs=[("v", 1)])
                    ld = p.dma("sync", x1t[:], x1_t[Q], "dr",
                               waits=[("dv", p.val("dv"))])
                    STT("vector", x2t[:], aO[:, qt * D:(qt + 1) * D], linv[:],
                        x1t[:], ALU.mult, ALU.add,
                        waits=[("dr", ld), ("v", p.val("v")),
                               ("dv", p.val("dv"))], incs=[("v", 1)])
                    p.dma("gpsimd", x2_t[Q], x2t[:], "dv",
                          waits=[("v", p.val("v"))])

            flat = [(b, qg, kb, qt) for b, (qg, kb) in enumerate(blocks)
                    for qt in range(QG)]
            NP = len(flat)
            entered = set()

            def block_entry(b):
                if b in entered:
                    return
                entered.add(b)
                qg, kb = blocks[b]
                if kb == 0:
                    qo = qg * DCH * TT
                    p.dma("scalar", aQh[:],
                          qth2[:, qo:qo + DCH * TT], "ds",
                          waits=[("t", p.val("t")), ("qs0", prepass_qs0),
                                 ("qs1", prepass_qs1)])
                    qld[qg] = p.dma("scalar", aQl[:],
                                    qtl2[:, qo:qo + DCH * TT], "ds")

            def lead_S(j):
                if j >= NP:
                    return
                bj, qgj, kbj, qtj = flat[j]
                block_entry(bj)
                fsc_s[j] = fscb[j % 2]
                emit_S(j, qtj, 0, kbj, qgj, enc_ld[bj], qld[qgj], bj)

            def memset_aO(qg):
                MEMSET("vector", aO[:], 0.0,
                       waits=[("v", p.val("v")), ("t", p.val("t"))],
                       incs=[("v", 1)])
                aO_inits[qg] = p.val("v")

            def post_merge(j):
                bj, qgj, kbj, qtj = flat[j]
                emit_merge(j, qtj, kbj, qgj, aO_inits[qgj])
                if qtj == QG - 1 and kbj == qgj:
                    finalize(qgj)
                    if j + 1 < NP:
                        memset_aO(flat[j + 1][1])

            prefetch_F(0)
            prefetch_T(0)
            prefetch_F(1)
            prefetch_T(1)
            lead_S(0)
            memset_aO(0)
            emit_VSP(0, flat[0][3], 0, flat[0][2], flat[0][1])
            for i in range(NP):
                bi, qgi, kbi, qti = flat[i]
                emit_TRG_mm(i)
                if i > 0:
                    post_merge(i - 1)
                emit_PT_evac(i)
                lead_S(i + 1)
                if i + 2 < NP:
                    block_entry(flat[i + 2][0])
                if i + 1 < NP and flat[i + 1][3] == QG - 1:
                    bn = flat[i + 1][0]
                    blk_s_done[bn] = s_done_t[i + 1]
                    prefetch_F(bn + 2)
                emit_AV(i, bi)
                if qti == QG - 1:
                    blk_mm_done[bi] = p.val("t")
                    prefetch_T(bi + 2)
                if i + 1 < NP:
                    bn, qgn, kbn, qtn = flat[i + 1]
                    emit_VSP(i + 1, qtn, 0, kbn, qgn)
            post_merge(NP - 1)

        ffn_call(x0, x1b.ap(), cidx=0, wload_gu=None,
                 pre_waits=[("dx", init_dx)], prep=prep_tiles)
        ffn1_o = p.val("o")

        if upto == "ffn1":
            for nt in range(NT):
                p.dma("gpsimd",
                      out.rearrange("(n q) d -> n q d", q=128)[nt],
                      x1b.ap().rearrange("(n q) d -> n q d", q=128)[nt],
                      "dg", waits=[("o", ffn1_o)] if nt == 0 else ())
            p.wait("gpsimd", [("dg", p.val("dg"))])
        else:
            attention()
            attn_v = p.val("v")
            attn_t = p.val("t")
            if upto == "attn":
                for nt in range(NT):
                    p.dma("gpsimd",
                          out.rearrange("(n q) d -> n q d", q=128)[nt],
                          x2b.ap().rearrange("(n q) d -> n q d", q=128)[nt],
                          "dg", waits=[("dv", p.val("dv"))] if nt == 0 else ())
                p.wait("gpsimd", [("dg", p.val("dg"))])
            else:
                p.dma("sync", WgT[:], wgsp.ap(), "dw",
                      waits=[("t", attn_t), ("v", attn_v)])
                p.dma("sync", WuT[:], wusp.ap(), "dw")
                wload2 = p.val("dw")
                ffn_call(x2b.ap(), out, cidx=1, wload_gu=wload2,
                         pre_waits=[("dv", p.val("dv")), ("dv0", p.val("dv0")),
                                    ("dv1", p.val("dv1")), ("ds", p.val("ds")),
                                    ("dg", p.val("dg")), ("dr", p.val("dr")),
                                    ("drg", p.val("drg")), ("dx", p.val("dx"))])
                p.wait("gpsimd", [("o", p.val("o"))])

        with nc.Block() as block:
            def emit_for(engine_name):
                def run(eng):
                    for fn, waits, incs in p.ops[engine_name]:
                        for s, v in waits:
                            eng.wait_ge(sems[s], v)
                        if fn is None:
                            continue
                        inst = fn(eng)
                        for s, n in incs:
                            inst.then_inc(sems[s], n)
                return run
            block.sync(emit_for("sync"))
            block.vector(emit_for("vector"))
            block.scalar(emit_for("scalar"))
            block.tensor(emit_for("tensor"))
            block.gpsimd(emit_for("gpsimd"))

    return nc


def _host_inputs(inputs, core):
    x = np.ascontiguousarray(np.asarray(inputs["L0_attn_encoding"])[core])
    m = {
        "x0": x.astype(np.float32, copy=False),
        "Rm": np.asarray(inputs["L0_attn_R"], dtype=np.float32),
        "wg": np.asarray(inputs["L0_ffn_w_gate"], dtype=np.float32),
        "wu": np.asarray(inputs["L0_ffn_w_up"], dtype=np.float32),
        "wd": np.asarray(inputs["L0_ffn_w_down"], dtype=np.float32),
        "clip_x": np.asarray(inputs["L0_ffn_clip_x"], dtype=np.float32),
        "clip_d": np.asarray(inputs["L0_ffn_clip_down"], dtype=np.float32),
        "id32": np.eye(128, dtype=np.float32),
        "id16": np.eye(128, dtype=np.float16),
        "idb16": np.eye(128).astype(ml_dtypes.bfloat16),
    }
    jj = np.arange(896)[None, :]
    pp = np.arange(128)[:, None]
    m["maskm"] = np.where(jj <= pp + 384, 0.0, -1e9).astype(np.float32)
    return m


def kernel(**inputs):
    upto = inputs.pop("_upto", "full")
    nc = build_program(upto=upto)
    in_maps = [_host_inputs(inputs, b) for b in range(NCORES)]
    res = run_bass_kernel_spmd(nc, in_maps, core_ids=list(range(NCORES)))
    return np.stack([np.asarray(res.results[b]["out"]) for b in range(NCORES)],
                    axis=0)


# version 30 (final)
# speedup vs baseline: 1.0127x; 1.0022x over previous
import numpy as np
import ml_dtypes
from contextlib import ExitStack

import concourse.bass as bass
import concourse.mybir as mybir
from concourse.ap import AP
from concourse.bass_utils import run_bass_kernel_spmd

F32 = mybir.dt.float32
F16 = mybir.dt.float16
BF16 = mybir.dt.bfloat16
I32 = mybir.dt.int32
U8 = mybir.dt.uint8
ALU = mybir.AluOpType
ACT = mybir.ActivationFunctionType
AX = mybir.AxisListType.X

NCORES = 8
NTOK = 4096
D = 1024
DI = 2816
NT = NTOK // 128
TT = 512
NG = NTOK // TT
DCH = D // 128
ICH = DI // 128
KQW = 1024
QG = 4
NQG = NT // QG

ENGINES = ("sync", "vector", "scalar", "tensor", "gpsimd")


class Plan:
    def __init__(self):
        self.ops = {e: [] for e in ENGINES}
        self.cnt = {}

    def inc(self, sem, n):
        self.cnt[sem] = self.cnt.get(sem, 0) + n
        return self.cnt[sem]

    def val(self, sem):
        return self.cnt.get(sem, 0)

    def op(self, engine, fn, waits=(), incs=()):
        wl = [(s, v) for (s, v) in waits if v > 0]
        il = []
        for s, n in incs:
            self.inc(s, n)
            il.append((s, n))
        self.ops[engine].append((fn, wl, il))

    def dma(self, queue, out, in_, sem, waits=()):
        self.op(queue, lambda eng, o=out, i=in_: eng.dma_start(out=o, in_=i),
                waits=waits, incs=[(sem, 16)])
        return self.val(sem)

    def wait(self, engine, waits):
        self.op(engine, None, waits=waits)


def _bcast_ap(dram_handle, nfree, npart=128):
    a = dram_handle.ap()
    return AP(tensor=a.tensor, offset=a.offset, ap=[[0, npart], [1, nfree]])


def build_program(upto="full"):
    nc = bass.Bass("TRN2", target_bir_lowering=False, debug=False)

    x0 = nc.dram_tensor("x0", [NTOK, D], F32, kind="ExternalInput").ap()
    Rm = nc.dram_tensor("Rm", [D, D], F32, kind="ExternalInput").ap()
    wg_d = nc.dram_tensor("wg", [DI, D], F32, kind="ExternalInput").ap()
    wu_d = nc.dram_tensor("wu", [DI, D], F32, kind="ExternalInput").ap()
    wd_d = nc.dram_tensor("wd", [D, DI], F32, kind="ExternalInput").ap()
    clip_x = nc.dram_tensor("clip_x", [1], F32, kind="ExternalInput")
    clip_d = nc.dram_tensor("clip_d", [1], F32, kind="ExternalInput")
    id32 = nc.dram_tensor("id32", [128, 128], F32, kind="ExternalInput").ap()
    id16 = nc.dram_tensor("id16", [128, 128], F16, kind="ExternalInput").ap()
    idb16 = nc.dram_tensor("idb16", [128, 128], BF16, kind="ExternalInput").ap()
    maskm = nc.dram_tensor("maskm", [128, 896], F32, kind="ExternalInput").ap()
    out = nc.dram_tensor("out", [NTOK, D], F32, kind="ExternalOutput").ap()

    x1b = nc.dram_tensor("x1b", [NTOK, D], F32)
    x2b = nc.dram_tensor("x2b", [NTOK, D], F32)
    hbuf = nc.dram_tensor("hbuf", [ICH, NG, 128, TT], F32)
    wgsp = nc.dram_tensor("wgsp", [128, DCH * DI], BF16)
    wusp = nc.dram_tensor("wusp", [128, DCH * DI], BF16)
    wdsp = nc.dram_tensor("wdsp", [128, ICH * D], BF16)
    encTh_d = nc.dram_tensor("encTh_d", [NTOK, D], F16)
    encTl_d = nc.dram_tensor("encTl_d", [NTOK, D], F16)
    encFh_d = nc.dram_tensor("encFh_d", [D, NTOK], F16)
    encFl_d = nc.dram_tensor("encFl_d", [D, NTOK], F16)
    qth_d = nc.dram_tensor("qth_d", [128, NQG * DCH * TT], F16)
    qtl_d = nc.dram_tensor("qtl_d", [128, NQG * DCH * TT], F16)
    cc_in = [nc.dram_tensor(f"cc_in{i}", [1], F32) for i in range(4)]
    cc_out = [nc.dram_tensor(f"cc_out{i}", [NCORES], F32) for i in range(4)]

    p = Plan()
    es = ExitStack()
    with es:
        sbt = lambda name, shape, dt: es.enter_context(nc.sbuf_tensor(name, shape, dt))

        c_id32 = sbt("c_id32", [128, 128], F32)
        c_id16 = sbt("c_id16", [128, 128], F16)
        c_idb16 = sbt("c_idb16", [128, 128], BF16)
        c_ones16 = sbt("c_ones16", [128, 128], F16)
        c_mask = sbt("c_mask", [128, 896], F32)
        sig_x = sbt("sig_x", [128, 1], F32)
        sig_d = sbt("sig_d", [128, 1], F32)
        c127 = sbt("c127", [128, 1], F32)
        r127 = sbt("r127", [128, 1], F32)
        eps_c = sbt("eps_c", [128, 1], F32)
        c256 = sbt("c256", [128, 1], F32)
        sWg = sbt("sWg", [128, ICH], F32)
        sWu = sbt("sWu", [128, ICH], F32)
        sWd = sbt("sWd", [128, DCH], F32)
        ag = sbt("ag", [128, ICH], F32)
        au = sbt("au", [128, ICH], F32)
        adh = sbt("adh", [128, DCH], F32)
        m_col = sbt("m_col", [128, NT], F32)
        nm_col = sbt("nm_col", [128, NT], F32)
        rs_col = sbt("rs_col", [128, NT], F32)
        krx_col = sbt("krx_col", [128, NT], F32)
        amax_run = sbt("amax_run", [128, 1], F32)
        t0c = sbt("t0c", [128, NT], F32)
        t1c = sbt("t1c", [128, NT], F32)
        habs_run = sbt("habs_run", [128, 1], F32)
        habs_tmp = sbt("habs_tmp", [128, 1], F32)
        ccg = sbt("ccg", [128, NCORES], F32)
        sxv = sbt("sxv", [128, 1], F32)
        rxv = sbt("rxv", [128, 1], F32)
        shv = sbt("shv", [128, 1], F32)
        rhv = sbt("rhv", [128, 1], F32)
        t0 = sbt("t0", [128, 1], F32)
        t1 = sbt("t1", [128, 1], F32)
        t3 = sbt("t3", [1, 1], F32)
        lsum_t = sbt("lsum_t", [128, 1], F32)
        lrun = sbt("lrun", [128, NT], F32)
        mrun = sbt("mrun", [128, NT], F32)
        mt = sbt("mt", [128, 1], F32)
        mnew = sbt("mnew", [128, 1], F32)
        nmnew = sbt("nmnew", [128, 1], F32)
        fscb = [sbt(f"fsc{i}", [128, 1], F32) for i in range(2)]
        linv = sbt("linv", [128, 1], F32)

        SLOT = 49152
        arenaA = sbt("arenaA", [128, 2 * SLOT], U8)
        aA = nc.lookup_mloc(arenaA).addr

        def alias(name, shape, dt, byte_off, base=None):
            h = nc.virtual_sbuf_tensor(name, shape, dt)
            nc._tensor(h.name, list(h.shape), h.dtype, type="SB")
            mloc = nc.lookup_mloc(h)
            mloc.allocated = True
            mloc.addr = (aA if base is None else base) + byte_off
            return h

        WgT = alias("WgT", [128, DCH * DI], BF16, 0)
        wtileb = [alias(f"wtile{i}", [128, DI], F32, i * 12288) for i in range(2)]
        wi32b = [alias(f"wi32{i}", [128, DI], I32, 24576 + i * 12288) for i in range(2)]
        wcfb = [alias(f"wcf{i}", [128, DI], F32, 49152 + i * 12288) for i in range(2)]
        wq16b = [alias(f"wq16{i}", [128, DI], BF16, 73728 + i * 8192) for i in range(2)]
        evb = alias("evb", [128, 1024], BF16, 90112)
        WuT = alias("WuT", [128, DCH * DI], BF16, SLOT)
        WdT = alias("WdT", [128, ICH * D], BF16, 0)
        aFhb = [alias(f"aFh{i}", [128, DCH * 512], F16, i * 8192)
                for i in range(2)]
        aFlb = [alias(f"aFl{i}", [128, DCH * 512], F16, 16384 + i * 8192)
                for i in range(2)]
        aThb = [alias(f"aTh{i}", [128, 4 * D], F16, 32768 + i * 8192)
                for i in range(2)]
        aTlb = [alias(f"aTl{i}", [128, 4 * D], F16, 49152 + i * 8192)
                for i in range(2)]
        aQh = alias("aQh", [128, DCH * (QG * 128)], F16, 65536)
        aQl = alias("aQl", [128, DCH * (QG * 128)], F16, 73728)
        aO = alias("aO", [128, QG * D], F32, 81920)
        aRh = alias("aRh", [128, DCH * D], F16, 0)
        aRl = alias("aRl", [128, DCH * D], F16, 16384)
        qhb = [alias(f"qh{i}", [128, ICH * TT], BF16, SLOT + i * 22528)
               for i in range(2)]

        arenaB = sbt("arenaB", [128, 104448], U8)
        aB = nc.lookup_mloc(arenaB).addr
        off = [0]

        def aliasB(name, shape, dt, at=None):
            if at is not None:
                off[0] = at
            nbytes = int(np.prod(shape[1:])) * mybir.dt.size(dt)
            h = alias(name, shape, dt, off[0], base=aB)
            off[0] += (nbytes + 31) & ~31
            return h

        xa = [aliasB(f"xa{i}", [128, D], F32) for i in range(4)]
        vq = aliasB("vq", [128, D], F32)
        iq = alias("iq", [128, D], I32, 94208)
        qxtok = [aliasB(f"qxtok{i}", [128, D], BF16) for i in range(4)]
        qxTb = [aliasB(f"qxT{i}", [128, DCH * TT], BF16) for i in range(2)]
        silu_t = aliasB("silu_t", [128, TT], F32)
        h_t = [aliasB(f"h_t{i}", [128, TT], F32) for i in range(3)]
        sqs_at = off[0]
        sqs = aliasB("sqs", [128, D], F32)
        cb = aliasB("cb", [128, D], F32, at=sqs_at)
        down_at = off[0]
        hin = [aliasB(f"hin{i}", [128, TT], F32) for i in range(6)]
        hvw = [aliasB(f"hvw{i}", [128, 2 * TT], F32) for i in range(2)]
        hvb = [hvw[i][:, 0:TT] for i in range(2)]
        hvb2 = [hvw[i][:, TT:2 * TT] for i in range(2)]
        hi32one = aliasB("hi32", [128, 2 * TT], I32)
        hi32b = [hi32one, hi32one]
        hcf1 = aliasB("hcf", [128, 2 * TT], F32)
        hcfb = [hcf1, hcf1]
        dsc = aliasB("dsc", [128, TT], F32)
        xout = aliasB("xout", [128, 4 * D], F32)
        ffn_end = off[0]
        wtile = aliasB("wtile", [128, DI], F32, at=down_at)
        wi32 = aliasB("wi32", [128, DI], I32)
        wcf = aliasB("wcf", [128, DI], F32)
        wq16 = aliasB("wq16", [128, DI], BF16)
        assert off[0] <= 104448, off[0]
        yab = [aliasB(f"ya{i}", [128, D], F32, at=0 if i == 0 else None)
               for i in range(2)]
        ya = yab[0]
        yhb = [aliasB(f"yh{i}", [128, D], F16) for i in range(2)]
        ylb = [aliasB(f"yl{i}", [128, D], F16) for i in range(2)]
        stg_h = aliasB("stg_h", [128, DCH * TT], F16)
        stg_l = aliasB("stg_l", [128, DCH * TT], F16)
        Pb = aliasB("Pb", [128, TT], F32)
        PTh = aliasB("PTh", [128, TT], F16)
        PTl = aliasB("PTl", [128, TT], F16)
        x1t = aliasB("x1t", [128, D], F32)
        x2t = aliasB("x2t", [128, D], F32)
        xfb = [x1t,
               alias("xf1", [128, D], F32, 16384, base=aB),
               alias("xf2", [128, D], F32, 20480, base=aB),
               alias("xf3", [128, D], F32, 24576, base=aB)]
        qsq = aliasB("qsq", [128, TT], F16)
        bnd = aliasB("bnd", [128, TT], F32)
        qevhb = [aliasB(f"qevh{i}", [128, TT], F16) for i in range(2)]
        qevlb = [aliasB(f"qevl{i}", [128, TT], F16) for i in range(2)]
        sq2 = aliasB("sq2", [128, D], F32)
        assert off[0] <= 104448, off[0]

        pA = es.enter_context(nc.psum_tensor("pA", [128, 512], F32))
        pB = es.enter_context(nc.psum_tensor("pB", [128, 512], F32))
        pC = es.enter_context(nc.psum_tensor("pC", [128, 512], F32))
        pD = es.enter_context(nc.psum_tensor("pD", [128, 512], F32))
        pO = es.enter_context(nc.psum_tensor("pO", [128, 1024], F32))
        pFb = es.enter_context(nc.psum_tensor("pFb", [128, 1024], BF16))
        pFh = es.enter_context(nc.psum_tensor("pFh", [128, 1024], F16))

        sem_names = (["dx", "dw", "dw0", "dw1", "wsp", "dg", "ds", "dv", "dv0",
                      "dv1", "dr", "drg", "v", "s", "t", "g", "cc", "o"]
                     + ["eA", "eB", "fA", "fB", "dr2", "qs0", "qs1"]
                     + [f"xa{i}" for i in range(4)]
                     + [f"hs{i}" for i in range(4)]
                     + [f"hi{i}" for i in range(8)])
        sems = {n: es.enter_context(nc.semaphore(n)) for n in sem_names}

        def TS(eng, outap, inap, s1, s2, op0, op1=None, waits=(), incs=(), accum=None):
            def fn(e, o=outap, i=inap, a=s1, b=s2, x=op0, y=op1, ac=accum):
                kw = {"accum_out": ac} if ac is not None else {}
                if y is None:
                    return e.tensor_scalar(o, i, a, None, x, **kw)
                return e.tensor_scalar(o, i, a, b, x, y, **kw)
            p.op(eng, fn, waits=waits, incs=incs)

        def TT_(eng, outap, a, b, aop, waits=(), incs=()):
            p.op(eng, lambda e, o=outap, x=a, y=b, op=aop: e.tensor_tensor(o, x, y, op),
                 waits=waits, incs=incs)

        def COPY(eng, outap, inap, waits=(), incs=()):
            p.op(eng, lambda e, o=outap, i=inap: e.tensor_copy(o, i),
                 waits=waits, incs=incs)

        def RED(eng, outap, inap, aop, absval=False, waits=(), incs=()):
            p.op(eng, lambda e, o=outap, i=inap, op=aop, a=absval:
                 e.tensor_reduce(o, i, AX, op, apply_absolute_value=a),
                 waits=waits, incs=incs)

        def RECIP(eng, outap, inap, waits=(), incs=()):
            p.op(eng, lambda e, o=outap, i=inap: e.reciprocal(o, i),
                 waits=waits, incs=incs)

        def ACTI(outap, inap, func, bias=0.0, scale=1.0, accum=None, waits=(), incs=()):
            def fn(e, o=outap, i=inap, f=func, b=bias, s=scale, a=accum):
                kw = {"accum_out": a} if a is not None else {}
                return e.activation(o, i, f, bias=b, scale=s, **kw)
            p.op("scalar", fn, waits=waits, incs=incs)

        def STT(eng, outap, in0, scalar, in1, op0, op1, waits=(), incs=()):
            p.op(eng, lambda e, o=outap, a=in0, s=scalar, b=in1, x=op0, y=op1:
                 e.scalar_tensor_tensor(o, a, s, b, x, y), waits=waits, incs=incs)

        def MM(outap, lhsT, rhs, start, stop, waits=(), incs=(), tp=False):
            p.op("tensor", lambda e, o=outap, l=lhsT, r=rhs, st=start, sp=stop, t=tp:
                 e.matmul(o, l, r, start=st, stop=sp, is_transpose=t),
                 waits=waits, incs=incs)

        def TRG(outap, inap, ident, start, stop, waits=(), incs=()):
            MM(outap, inap, ident, start, stop, waits=waits, incs=incs, tp=True)

        def MEMSET(eng, ap_, const, waits=(), incs=()):
            p.op(eng, lambda e, a=ap_, c=const: e.memset(a, c),
                 waits=waits, incs=incs)

        p.dma("sync", c_id32[:], id32, "dx")
        p.dma("sync", c_id16[:], id16, "dx")
        p.dma("sync", c_idb16[:], idb16, "dx")
        p.dma("sync", c_mask[:], maskm, "dx")
        p.dma("sync", sig_x[:], _bcast_ap(clip_x, 1), "dx")
        p.dma("sync", sig_d[:], _bcast_ap(clip_d, 1), "dx")
        init_dx = p.val("dx")
        MEMSET("vector", c_ones16[:], 1.0, incs=[("v", 1)])
        MEMSET("vector", c127[:], 127.0, incs=[("v", 1)])
        MEMSET("vector", eps_c[:], 1e-5, incs=[("v", 1)])
        MEMSET("vector", c256[:], 256.0, incs=[("v", 1)])
        nc.const_aps.register(F32, 256.0, c256[:])
        RECIP("vector", r127[:], c127[:], waits=[("v", p.val("v"))], incs=[("v", 1)])
        ACTI(sig_x[:], sig_x[:], ACT.Sigmoid, waits=[("dx", init_dx)], incs=[("s", 1)])
        ACTI(sig_d[:], sig_d[:], ACT.Sigmoid, incs=[("s", 1)])
        init_s = p.val("s")
        p.wait("tensor", [("dx", init_dx)])
        p.wait("vector", [("dx", init_dx)])
        p.wait("gpsimd", [("dx", init_dx)])

        prep_free = {}

        def prep_weight(src_ap, ntiles, freew, sW_t, spill, base):
            nch = freew // 128
            sp3 = spill.ap().rearrange("p (c n) -> p c n", c=nch)
            ev3 = evb.ap().rearrange("p (c n) -> p c n", c=8)
            tiles = []
            for it in range(ntiles):
                def emit(it=it):
                    gt = base + it
                    pp = gt % 2
                    wtile = wtileb[pp]
                    wi32 = wi32b[pp]
                    wcf = wcfb[pp]
                    wq16 = wq16b[pp]
                    sw_ = f"dw{pp}"
                    ld = p.dma("sync", wtile[:, 0:freew],
                               src_ap[it * 128:(it + 1) * 128, :], sw_,
                               waits=[("v", prep_free.get(gt - 2, 0))])
                    RED("vector", t0[:], wtile[:, 0:freew], ALU.max,
                        absval=True,
                        waits=[(sw_, ld), ("v", p.val("v"))], incs=[("v", 1)])
                    TT_("vector", sW_t[:, it:it + 1], t0[:], r127[:], ALU.mult,
                        waits=[("v", p.val("v"))], incs=[("v", 1)])
                    RECIP("vector", t1[:], sW_t[:, it:it + 1],
                          waits=[("v", p.val("v"))], incs=[("v", 1)])
                    TS("vector", wtile[:, 0:freew], wtile[:, 0:freew], t1[:],
                       256.0, ALU.mult, ALU.add,
                       waits=[("v", p.val("v"))], incs=[("v", 1)])
                    p.op("scalar",
                         lambda e, o=wi32[:, 0:freew], i=wtile[:, 0:freew]:
                         e.copy(o, i),
                         waits=[("v", p.val("v")), ("s", p.val("s"))],
                         incs=[("s", 1)])
                    STT("vector", wcf[:, 0:freew], wi32[:, 0:freew], 0.5,
                        wtile[:, 0:freew], ALU.add, ALU.is_le,
                        waits=[("s", p.val("s")), ("v", p.val("v"))],
                        incs=[("v", 1)])
                    STT("vector", wq16[:, 0:freew], wi32[:, 0:freew], -256.0,
                        wcf[:, 0:freew], ALU.add, ALU.add,
                        waits=[("v", p.val("v"))], incs=[("v", 1)])
                    prep_free[gt] = p.val("v")
                    c = 0
                    while c < nch:
                        nb = min(8, nch - c)
                        t_w = p.val("t")
                        for j in range(nb):
                            TRG(pFb[:, j * 128:(j + 1) * 128],
                                wq16[:, (c + j) * 128:(c + j + 1) * 128],
                                c_idb16[:],
                                start=(j == 0), stop=(j == nb - 1),
                                waits=([("v", p.val("v")), ("t", t_w)]
                                       if j == 0 else ()),
                                incs=([("t", 1)] if j == nb - 1 else ()))
                        COPY("vector", ev3[:, 0:nb, :], pFb[:, 0:nb * 128],
                             waits=[("t", p.val("t")), ("wsp", p.val("wsp"))],
                             incs=[("v", 1)])
                        p.dma("gpsimd",
                              sp3[:, c:c + nb, it * 128:(it + 1) * 128],
                              ev3[:, 0:nb, :], "wsp", waits=[("v", p.val("v"))])
                        c += nb
                tiles.append(emit)
            return tiles

        prep_tiles = (prep_weight(wg_d, ICH, D, sWg, wgsp, 0)
                      + prep_weight(wu_d, ICH, D, sWu, wusp, ICH)
                      + prep_weight(wd_d, DCH, DI, sWd, wdsp, 2 * ICH))

        def preduce_max(in_col):
            TRG(pC[0:1, 0:128], in_col, c_id32[:], start=True, stop=True,
                waits=[("t", p.val("t")), ("v", p.val("v")), ("s", p.val("s"))],
                incs=[("t", 1)])
            RED("vector", t3[:], pC[0:1, 0:128], ALU.max,
                waits=[("t", p.val("t"))], incs=[("v", 1)])

        def collective(idx, out_col):
            st = p.dma("gpsimd", cc_in[idx].ap(), t3[:], "dg",
                       waits=[("v", p.val("v"))])

            def fn(e, i=idx):
                return e.collective_compute(
                    "AllGather", ALU.bypass,
                    replica_groups=[list(range(NCORES))],
                    ins=[cc_in[i].ap().opt()],
                    outs=[cc_out[i].ap().opt()],
                )
            p.op("gpsimd", fn, waits=[("dg", st)], incs=[("cc", 1)])
            ld = p.dma("gpsimd", ccg[:], _bcast_ap(cc_out[idx], NCORES), "dg",
                       waits=[("cc", p.val("cc"))])
            RED("vector", out_col, ccg[:], ALU.max, waits=[("dg", ld)],
                incs=[("v", 1)])

        def ffn_call(x_in, x_out, cidx, wload_gu, pre_waits=(), prep=None):
            xin_t = x_in.rearrange("(n q) d -> n q d", q=128)
            xout_t = x_out.rearrange("(n q) d -> n q d", q=128)
            xin_g = x_in.rearrange("(n q) d -> q n d", q=128)

            pa_done = {}
            ppi = 0
            for nt in range(NT):
                buf = xa[nt % 4]
                sx_ = f"xa{nt % 4}"
                pv, ps_ = pa_done.get(nt - 4, (0, 0))
                w = [("v", pv), ("s", ps_)]
                if nt < 4:
                    w += list(pre_waits)
                ld = p.dma("sync", buf[:], xin_t[nt], sx_, waits=w)
                RED("vector", t0c[:, nt:nt + 1], buf[:], ALU.max,
                    waits=[(sx_, ld), ("v", p.val("v"))], incs=[("v", 1)])
                RED("vector", t1c[:, nt:nt + 1], buf[:], ALU.min, incs=[("v", 1)])
                RED("vector", m_col[:, nt:nt + 1], buf[:], ALU.add, incs=[("v", 1)])
                TS("vector", m_col[:, nt:nt + 1], m_col[:, nt:nt + 1], 1.0 / D,
                   None, ALU.mult, waits=[("v", p.val("v"))], incs=[("v", 1)])
                TS("vector", nm_col[:, nt:nt + 1], m_col[:, nt:nt + 1], -1.0,
                   None, ALU.mult, waits=[("v", p.val("v"))], incs=[("v", 1)])
                ACTI(sqs[:], buf[:], ACT.Square, bias=nm_col[:, nt:nt + 1],
                     scale=1.0, accum=rs_col[:, nt:nt + 1],
                     waits=[("v", p.val("v")), ("s", p.val("s"))],
                     incs=[("s", 1)])
                ACTI(rs_col[:, nt:nt + 1], rs_col[:, nt:nt + 1], ACT.Sqrt,
                     bias=eps_c[:], scale=1.0 / D, waits=[("s", p.val("s"))],
                     incs=[("s", 1)])
                pa_done[nt] = (p.val("v"), p.val("s"))
                if prep:
                    tgt = (nt + 1) * len(prep) // NT
                    while ppi < tgt:
                        prep[ppi]()
                        ppi += 1
            if prep:
                p.dma("sync", WgT[:], wgsp.ap(), "dw",
                      waits=[("wsp", p.val("wsp"))])
                p.dma("sync", WuT[:], wusp.ap(), "dw")
                wload_gu = p.val("dw")
            RECIP("vector", rs_col[:], rs_col[:],
                  waits=[("s", p.val("s")), ("v", p.val("v"))], incs=[("v", 1)])
            TT_("vector", t0c[:], t0c[:], m_col[:], ALU.subtract,
                waits=[("v", p.val("v"))], incs=[("v", 1)])
            TT_("vector", t1c[:], m_col[:], t1c[:], ALU.subtract,
                incs=[("v", 1)])
            TT_("vector", t0c[:], t0c[:], t1c[:], ALU.max,
                waits=[("v", p.val("v"))], incs=[("v", 1)])
            TT_("vector", t0c[:], t0c[:], rs_col[:], ALU.mult,
                waits=[("v", p.val("v"))], incs=[("v", 1)])
            RED("vector", amax_run[:], t0c[:], ALU.max,
                waits=[("v", p.val("v"))], incs=[("v", 1)])
            preduce_max(amax_run[:])
            collective(2 * cidx, t0[:])
            TT_("vector", t0[:], t0[:], sig_x[:], ALU.mult,
                waits=[("v", p.val("v")), ("s", init_s)], incs=[("v", 1)])
            TT_("vector", sxv[:], t0[:], r127[:], ALU.mult,
                waits=[("v", p.val("v"))], incs=[("v", 1)])
            RECIP("vector", rxv[:], sxv[:], waits=[("v", p.val("v"))],
                  incs=[("v", 1)])
            TS("vector", krx_col[:], rs_col[:], rxv[:], None, ALU.mult,
               waits=[("v", p.val("v"))], incs=[("v", 1)])
            TS("vector", ag[:], sWg[:], sxv[:], None, ALU.mult,
               waits=[("v", p.val("v"))], incs=[("v", 1)])
            TS("vector", au[:], sWu[:], sxv[:], None, ALU.mult, incs=[("v", 1)])
            scales_v = p.val("v")

            MEMSET("vector", habs_run[:], 0.0, waits=[("v", p.val("v"))],
                   incs=[("v", 1)])
            wgT3 = WgT.ap().rearrange("p (c n) -> p c n", c=DCH)
            wuT3 = WuT.ap().rearrange("p (c n) -> p c n", c=DCH)
            pFb3 = pFb.ap().rearrange("p (c n) -> p c n", c=DCH)
            base_s = p.val("s")
            sil_done = {}
            stt_done = {}
            mm_done_g = {}
            xq_done = {}
            qdone_t = {}
            last_sp = {i: p.val(f"hs{i}") for i in range(3)}

            def quant_vec(g):
                for j in range(4):
                    nt = g * 4 + j
                    buf = xa[j]
                    sx_ = f"xa{j}"
                    ld = p.dma("sync", buf[:], xin_t[nt], sx_,
                               waits=[("v", scales_v if nt < 4
                                       else xq_done.get(nt - 4, 0))])
                    TS("vector", vq[:], buf[:], m_col[:, nt:nt + 1],
                       krx_col[:, nt:nt + 1], ALU.subtract, ALU.mult,
                       waits=[(sx_, ld), ("v", scales_v)], incs=[("v", 1)])
                    TS("vector", vq[:], vq[:], 127.0, -128.0, ALU.min, ALU.max,
                       waits=[("v", p.val("v"))], incs=[("v", 1)])
                    TS("vector", vq[:], vq[:], 256.0, None, ALU.add,
                       waits=[("v", p.val("v"))], incs=[("v", 1)])
                    p.op("scalar", lambda e, o=iq[:], i=vq[:]: e.copy(o, i),
                         waits=[("v", p.val("v")), ("s", p.val("s"))],
                         incs=[("s", 1)])
                    STT("vector", cb[:], iq[:], 0.5, vq[:], ALU.add, ALU.is_le,
                        waits=[("s", p.val("s"))], incs=[("v", 1)])
                    STT("vector", qxtok[j][:], iq[:], -256.0, cb[:],
                        ALU.add, ALU.add,
                        waits=[("v", p.val("v")),
                               ("t", qdone_t.get(g - 1, 0))], incs=[("v", 1)])
                    xq_done[nt] = p.val("v")

            def transp(g):
                qxT = qxTb[g % 2]
                qxT3 = qxT.ap().rearrange("p (c n) -> p c n", c=DCH)
                for j in range(4):
                    nt = g * 4 + j
                    t_w = p.val("t")
                    for k in range(DCH):
                        TRG(pFb[:, k * 128:(k + 1) * 128],
                            qxtok[j][:, k * 128:(k + 1) * 128], c_idb16[:],
                            start=(k == 0), stop=(k == DCH - 1),
                            waits=([("v", max(xq_done[nt], p.val("v"))),
                                    ("t", max(t_w, mm_done_g.get(g - 2, 0)))]
                                   if k == 0 else ()),
                            incs=([("t", 1)] if k == DCH - 1 else ()))
                    COPY("vector", qxT3[:, :, j * 128:(j + 1) * 128], pFb3,
                         waits=[("t", p.val("t"))], incs=[("v", 1)])
                qdone_t[g] = p.val("t")

            def iloop(g):
                qxT = qxTb[g % 2]
                qx_ready = p.val("v")
                base_v = qx_ready
                for i in range(ICH):
                    gi = g * ICH + i
                    pg = pA if gi % 2 == 0 else pC
                    pu = pB if gi % 2 == 0 else pD
                    for k in range(DCH):
                        MM(pg[:], wgT3[:, k, i * 128:(i + 1) * 128],
                           qxT[:, k * TT:(k + 1) * TT],
                           start=(k == 0), stop=(k == DCH - 1),
                           waits=([("v", qx_ready), ("dw", wload_gu),
                                   ("s", sil_done.get(gi - 2, base_s))]
                                  if k == 0 else ()),
                           incs=([("t", 1)] if k == DCH - 1 else ()))
                    ga_t = p.val("t")
                    for k in range(DCH):
                        MM(pu[:], wuT3[:, k, i * 128:(i + 1) * 128],
                           qxT[:, k * TT:(k + 1) * TT],
                           start=(k == 0), stop=(k == DCH - 1),
                           waits=([("v", stt_done.get(gi - 2, base_v))]
                                  if k == 0 else ()),
                           incs=([("t", 1)] if k == DCH - 1 else ()))
                    up_t = p.val("t")
                    ACTI(silu_t[:], pg[:], ACT.Silu, bias=0.0,
                         scale=ag[:, i:i + 1],
                         waits=[("t", ga_t), ("v", p.val("v"))], incs=[("s", 1)])
                    sil_done[gi] = p.val("s")
                    hb = h_t[gi % 3]
                    STT("vector", hb[:], pu[:], au[:, i:i + 1], silu_t[:],
                        ALU.mult, ALU.mult,
                        waits=[("t", up_t), ("s", p.val("s")),
                               (f"hs{gi % 3}", last_sp[gi % 3])],
                        incs=[("v", 1)])
                    stt_done[gi] = p.val("v")
                    hv_v = p.val("v")
                    RED("vector", habs_tmp[:], hb[:], ALU.max, absval=True,
                        waits=[("v", hv_v)], incs=[("v", 1)])
                    TT_("vector", habs_run[:], habs_run[:], habs_tmp[:], ALU.max,
                        waits=[("v", p.val("v"))], incs=[("v", 1)])
                    p.dma("gpsimd", hbuf.ap()[i, g], hb[:], f"hs{gi % 3}",
                          waits=[("v", hv_v)])
                    last_sp[gi % 3] = p.val(f"hs{gi % 3}")
                mm_done_g[g] = p.val("t")

            quant_vec(0)
            transp(0)
            for g in range(NG):
                if g + 1 < NG:
                    quant_vec(g + 1)
                iloop(g)
                if g + 1 < NG:
                    transp(g + 1)

            preduce_max(habs_run[:])
            collective(2 * cidx + 1, t0[:])
            TT_("vector", t0[:], t0[:], sig_d[:], ALU.mult,
                waits=[("v", p.val("v"))], incs=[("v", 1)])
            TT_("vector", shv[:], t0[:], r127[:], ALU.mult,
                waits=[("v", p.val("v"))], incs=[("v", 1)])
            RECIP("vector", rhv[:], shv[:], waits=[("v", p.val("v"))],
                  incs=[("v", 1)])
            TS("vector", adh[:], sWd[:], shv[:], 0.5, ALU.mult, ALU.mult,
               waits=[("v", p.val("v"))], incs=[("v", 1)])
            hscales_v = p.val("v")

            p.dma("sync", WdT[:], wdsp.ap(), "dw", waits=[("t", p.val("t"))])
            wload_d = p.val("dw")

            wdT3 = WdT.ap().rearrange("p (c n) -> p c n", c=ICH)
            xo4 = xout.ap().rearrange("p (j d) -> p j d", j=4)
            p.wait("sync", [(f"hs{i}", p.val(f"hs{i}")) for i in range(4)])
            dsc_done = {}
            dmm_done = {}
            hin_free = {}
            hv_free = {}
            qh_ready = {}

            def quant_group(g):
                qh = qhb[g % 2]
                qh3 = qh.ap().rearrange("p (c n) -> p c n", c=ICH)
                mm_prev = dmm_done.get(g - 2, p.val("t") if g == 0 else 0)
                for ci in range(ICH // 2):
                    c = 2 * ci
                    gci = g * (ICH // 2) + ci
                    pp = gci % 2
                    p8 = gci % 3
                    buf = hin[2 * p8]
                    buf2 = hin[2 * p8 + 1]
                    hv = hvb[pp]
                    hv2 = hvb2[pp]
                    hi32 = hi32b[pp]
                    hcf = hcfb[pp]
                    sh_ = f"hi{2 * p8}"
                    sh2_ = f"hi{2 * p8 + 1}"
                    ld1 = p.dma("sync", buf[:], hbuf.ap()[c, g], sh_,
                                waits=[("s", hin_free.get((gci - 3, 0), 0))])
                    ld2 = p.dma("sync", buf2[:], hbuf.ap()[c + 1, g], sh2_,
                                waits=[("s", hin_free.get((gci - 3, 1), 0))])
                    ACTI(hv, buf[:], ACT.Copy, bias=256.0, scale=rhv[:],
                         waits=[(sh_, ld1),
                                ("v", hscales_v if g == 0 and ci == 0
                                 else hv_free.get(gci - 2, 0)),
                                ("t", mm_prev)], incs=[("s", 1)])
                    hin_free[(gci, 0)] = p.val("s")
                    ACTI(hv2, buf2[:], ACT.Copy, bias=256.0, scale=rhv[:],
                         waits=[(sh2_, ld2)], incs=[("s", 1)])
                    hin_free[(gci, 1)] = p.val("s")
                    TS("vector", hvw[pp][:], hvw[pp][:], 383.0, 128.0,
                       ALU.min, ALU.max,
                       waits=[("s", p.val("s")), ("v", p.val("v"))],
                       incs=[("v", 1)])
                    p.op("scalar", lambda e, o=hi32[:], i=hvw[pp][:]:
                         e.copy(o, i),
                         waits=[("v", p.val("v"))], incs=[("s", 1)])
                    STT("vector", hcf[:], hi32[:], 0.5, hvw[pp][:], ALU.add,
                        ALU.is_le, waits=[("s", p.val("s")),
                                          ("v", p.val("v"))], incs=[("v", 1)])
                    hv_free[gci] = p.val("v")
                    STT("vector", qh3[:, c:c + 2, :], hi32[:], -256.0, hcf[:],
                        ALU.add, ALU.add,
                        waits=[("v", p.val("v"))], incs=[("v", 1)])
                qh_ready[g] = p.val("v")

            def down_group(g):
                qh = qhb[g % 2]
                qh3 = qh.ap().rearrange("p (c n) -> p c n", c=ICH)
                for j in range(4):
                    xld = p.dma("gpsimd", xo4[:, j, :],
                                xin_g[:, g * 4 + j, :], "drg",
                                waits=([("o", p.val("o"))] if j == 0 else ()))
                tr_done = {}
                for hc in range(DCH):
                    pd_ = pA if hc % 2 == 0 else pC
                    for c in range(ICH):
                        MM(pd_[:], wdT3[:, c, hc * 128:(hc + 1) * 128],
                           qh3[:, c, :], start=(c == 0), stop=(c == ICH - 1),
                           waits=([("v", dsc_done.get((g, hc - 2),
                                        qh_ready[g])),
                                   ("dw", wload_d)] if c == 0 else ()),
                           incs=([("t", 1)] if c == ICH - 1 else ()))
                    dmm = p.val("t")
                    if hc > 0:
                        emit_epi(g, hc - 1, tr_done)
                    tr_done[hc] = dmm
                emit_epi(g, DCH - 1, tr_done)
                dmm_done[g] = p.val("t")
                res_v = p.val("v")
                for j in range(4):
                    p.dma("gpsimd", xout_t[g * 4 + j],
                          xout[:, j * D:(j + 1) * D], "o",
                          waits=[("v", res_v)] if j == 0 else ())

            def emit_epi(g, hc, tr_done):
                pd_ = pA if hc % 2 == 0 else pC
                ptr_ = pB if hc % 2 == 0 else pD
                TS("vector", dsc[:], pd_[:], adh[:, hc:hc + 1], None,
                   ALU.mult, waits=[("t", tr_done[hc]), ("v", p.val("v"))],
                   incs=[("v", 1)])
                dsc_done[(g, hc)] = p.val("v")
                t_w2 = p.val("t")
                for j in range(4):
                    TRG(ptr_[:, j * 128:(j + 1) * 128],
                        dsc[:, j * 128:(j + 1) * 128], c_id32[:],
                        start=(j == 0), stop=(j == 3),
                        waits=([("v", dsc_done[(g, hc)]), ("t", t_w2)]
                               if j == 0 else ()),
                        incs=([("t", 1)] if j == 3 else ()))
                tr_t = p.val("t")
                for j in range(4):
                    TT_("vector",
                        xout[:, j * D + hc * 128: j * D + (hc + 1) * 128],
                        ptr_[:, j * 128:(j + 1) * 128],
                        xout[:, j * D + hc * 128: j * D + (hc + 1) * 128],
                        ALU.add,
                        waits=([("t", tr_t), ("drg", p.val("drg"))]
                               if j == 0 else ()),
                        incs=([("v", 1)] if j == 3 else ()))

            quant_group(0)
            for g in range(NG):
                if g + 1 < NG:
                    quant_group(g + 1)
                down_group(g)
        def attention():
            x1_t = x1b.ap().rearrange("(n q) d -> n q d", q=128)
            x2_t = x2b.ap().rearrange("(n q) d -> n q d", q=128)
            eTh4 = encTh_d.ap().rearrange("(n q) d -> n q d", q=128)
            eTl4 = encTl_d.ap().rearrange("(n q) d -> n q d", q=128)
            eThq = encTh_d.ap().rearrange("(c q) d -> q c d", q=128)
            eTlq = encTl_d.ap().rearrange("(c q) d -> q c d", q=128)
            eFh3 = encFh_d.ap().rearrange("(c q) n -> q c n", q=128)
            eFl3 = encFl_d.ap().rearrange("(c q) n -> q c n", q=128)
            qth2 = qth_d.ap()
            qtl2 = qtl_d.ap()
            stgh3 = stg_h.ap().rearrange("p (c n) -> p c n", c=DCH)
            stgl3 = stg_l.ap().rearrange("p (c n) -> p c n", c=DCH)
            aRh3 = aRh.ap().rearrange("p (c n) -> p c n", c=DCH)
            aRl3 = aRl.ap().rearrange("p (c n) -> p c n", c=DCH)
            pFh3 = pFh.ap().rearrange("p (c n) -> p c n", c=DCH)

            for k in range(DCH):
                ld = p.dma("sync", ya[:], Rm[k * 128:(k + 1) * 128, :], "dr",
                           waits=[("v", p.val("v")), ("t", p.val("t"))])
                COPY("vector", aRh3[:, k, :], ya[:], waits=[("dr", ld)],
                     incs=[("v", 1)])
                TT_("vector", aRl3[:, k, :], ya[:], aRh3[:, k, :], ALU.subtract,
                    waits=[("v", p.val("v"))], incs=[("v", 1)])
            r_ready = p.val("v")

            ya_free = {}
            y_free = {}
            dv_free = {}
            qev_done = {}
            qsp_done = {}
            for grp in range(NG):
                for j in range(4):
                    nt = grp * 4 + j
                    pp = nt % 2
                    ya_ = yab[pp]
                    yh = yhb[pp]
                    yl = ylb[pp]
                    sy_ = f"xa{pp}"
                    w = [("v", ya_free.get(nt - 2, 0)),
                         ("s", p.val("s") if nt < 2 else 0)]
                    if nt == 0:
                        w += [("o", p.val("o")), ("v", p.val("v"))]
                    ld = p.dma("sync", ya_[:], x1_t[nt], sy_, waits=w)
                    RED("vector", m_col[:, nt:nt + 1], ya_[:], ALU.add,
                        waits=[(sy_, ld), ("v", p.val("v"))], incs=[("v", 1)])
                    TS("vector", nm_col[:, nt:nt + 1], m_col[:, nt:nt + 1],
                       -1.0 / D, None, ALU.mult, waits=[("v", p.val("v"))],
                       incs=[("v", 1)])
                    ACTI(sq2[:], ya_[:], ACT.Square, bias=nm_col[:, nt:nt + 1],
                         scale=1.0, accum=rs_col[:, nt:nt + 1],
                         waits=[("v", p.val("v")), ("s", p.val("s"))],
                         incs=[("s", 1)])
                    ACTI(rs_col[:, nt:nt + 1], rs_col[:, nt:nt + 1], ACT.Sqrt,
                         bias=eps_c[:], scale=1.0 / D, waits=[("s", p.val("s"))],
                         incs=[("s", 1)])
                    RECIP("vector", rs_col[:, nt:nt + 1], rs_col[:, nt:nt + 1],
                          waits=[("s", p.val("s")), ("v", p.val("v"))],
                          incs=[("v", 1)])
                    TS("vector", ya_[:], ya_[:], nm_col[:, nt:nt + 1],
                       rs_col[:, nt:nt + 1], ALU.add, ALU.mult,
                       waits=[("v", p.val("v"))], incs=[("v", 1)])
                    COPY("vector", yh[:], ya_[:],
                         waits=[("v", max(p.val("v"), y_free.get(nt - 2, 0))),
                                (f"dv{pp}", dv_free.get(nt - 2, 0))],
                         incs=[("v", 1)])
                    TT_("vector", yl[:], ya_[:], yh[:], ALU.subtract,
                        waits=[("v", p.val("v"))], incs=[("v", 1)])
                    y_v = p.val("v")
                    ya_free[nt] = y_v
                    p.dma("gpsimd", eTh4[nt], yh[:], f"dv{pp}",
                          waits=[("v", y_v)])
                    p.dma("gpsimd", eTl4[nt], yl[:], f"dv{pp}")
                    dv_free[nt] = p.val(f"dv{pp}")
                    t_w = p.val("t")
                    for k in range(DCH):
                        TRG(pFh[:, k * 128:(k + 1) * 128],
                            yh[:, k * 128:(k + 1) * 128], c_id16[:],
                            start=(k == 0), stop=(k == DCH - 1),
                            waits=([("v", y_v), ("t", t_w)] if k == 0 else ()),
                            incs=([("t", 1)] if k == DCH - 1 else ()))
                    COPY("vector", stgh3[:, :, j * 128:(j + 1) * 128], pFh3,
                         waits=[("t", p.val("t")), ("dg", p.val("dg"))],
                         incs=[("v", 1)])
                    t_w = p.val("t")
                    for k in range(DCH):
                        TRG(pFh[:, k * 128:(k + 1) * 128],
                            yl[:, k * 128:(k + 1) * 128], c_id16[:],
                            start=(k == 0), stop=(k == DCH - 1),
                            waits=([("v", p.val("v")), ("t", t_w)] if k == 0
                                   else ()),
                            incs=([("t", 1)] if k == DCH - 1 else ()))
                    COPY("vector", stgl3[:, :, j * 128:(j + 1) * 128], pFh3,
                         waits=[("t", p.val("t"))], incs=[("v", 1)])
                    y_free[nt] = p.val("v")
                stg_v = p.val("v")
                p.dma("gpsimd", eFh3[:, :, grp * TT:(grp + 1) * TT], stgh3, "dg",
                      waits=[("v", stg_v)])
                p.dma("gpsimd", eFl3[:, :, grp * TT:(grp + 1) * TT], stgl3, "dg")
                for ec in range(DCH):
                    pq = pA if ec % 2 == 0 else pC
                    gec = grp * DCH + ec
                    n = 0
                    for (lh, rh) in ((aRh3, stgh3), (aRh3, stgl3), (aRl3, stgh3)):
                        for k in range(DCH):
                            n += 1
                            MM(pq[:], lh[:, k, ec * 128:(ec + 1) * 128],
                               rh[:, k, :], start=(n == 1), stop=(n == 24),
                               waits=([("v", max(stg_v, qev_done.get(
                                             gec - 2, 0))),
                                       ("s", p.val("s") if gec == 0 else 0),
                                       ("t", p.val("t") if gec == 0 else 0)]
                                      if n == 1 else ()),
                               incs=([("t", 1)] if n == 24 else ()))
                    qp_t = p.val("t")
                    qeh = qevhb[ec % 2]
                    qel = qevlb[ec % 2]
                    COPY("vector", qeh[:], pq[:],
                         waits=[("t", qp_t),
                                (f"qs{ec % 2}", qsp_done.get(gec - 2, 0)),
                                ("v", p.val("v"))], incs=[("v", 1)])
                    TT_("vector", qel[:], pq[:], qeh[:], ALU.subtract,
                        waits=[("v", p.val("v"))], incs=[("v", 1)])
                    qev_done[gec] = p.val("v")
                    q_v = p.val("v")
                    qsem = f"qs{ec % 2}"
                    qo = grp * DCH * TT + ec * TT
                    p.dma("scalar", qth2[:, qo:qo + TT],
                          qeh[:], qsem, waits=[("v", q_v)])
                    p.dma("scalar", qtl2[:, qo:qo + TT],
                          qel[:], qsem)
                    qsp_done[gec] = p.val(qsem)
            prepass_dv0 = p.val("dv0")
            prepass_dv1 = p.val("dv1")
            prepass_dg = p.val("dg")
            prepass_qs0 = p.val("qs0")
            prepass_qs1 = p.val("qs1")

            aQh3 = aQh.ap().rearrange("p (c n) -> p c n", c=DCH)
            aQl3 = aQl.ap().rearrange("p (c n) -> p c n", c=DCH)
            MEMSET("vector", lrun[:], 0.0, waits=[("v", p.val("v"))],
                   incs=[("v", 1)])
            MEMSET("vector", mrun[:], -1e30, incs=[("v", 1)])
            s_bank_free = {}
            pt_bank_free = [0]
            pb_free_t = [0]
            s_done_t = {}
            av_done_t = {}
            pt_done_v = {}

            def emit_S(pi, qt, kt, kb, qg, eld, ql, b):
                ps = (pA, pC, pD)[pi % 3]
                sfree = s_bank_free.get(pi % 3, (0, 0))
                aFh_ = aF3[0][b % 2]
                aFl_ = aF3[1][b % 2]
                n = 0
                for (lh, rh) in ((aQh3, aFh_), (aQh3, aFl_),
                                 (aQl3, aFh_)):
                    for k in range(DCH):
                        n += 1
                        MM(ps[:], lh[:, k, qt * 128:(qt + 1) * 128],
                           rh[:, k, 0:512],
                           start=(n == 1), stop=(n == 24),
                           waits=([eld, ("ds", ql),
                                   ("s", sfree[1]), ("v", sfree[0])]
                                  if n == 1 else ()),
                           incs=([("t", 1)] if n == 24 else ()))
                s_done_t[pi] = p.val("t")

            def emit_VSP(pi, qt, kt, kb, qg):
                ps = (pA, pC, pD)[pi % 3]
                Q = qg * QG + qt
                K2 = kb
                s_t = s_done_t[pi]
                if K2 == Q // 4:
                    offm = 128 * Q - 512 * K2
                    TT_("vector", ps[:], ps[:],
                        c_mask[:, 384 - offm:896 - offm], ALU.add,
                        waits=[("t", s_t), ("v", p.val("v"))], incs=[("v", 1)])
                RED("vector", mt[:], ps[:], ALU.max,
                    waits=[("t", s_t), ("v", p.val("v"))], incs=[("v", 1)])
                TT_("vector", mnew[:], mrun[:, Q:Q + 1], mt[:], ALU.max,
                    waits=[("v", p.val("v"))], incs=[("v", 1)])
                TS("vector", nmnew[:], mnew[:], -1.0, None, ALU.mult,
                   waits=[("v", p.val("v"))], incs=[("v", 1)])
                mv = p.val("v")
                ACTI(fsc_s[pi][:], mrun[:, Q:Q + 1], ACT.Exp, bias=nmnew[:],
                     scale=1.0, waits=[("v", mv), ("s", p.val("s"))],
                     incs=[("s", 1)])
                COPY("vector", mrun[:, Q:Q + 1], mnew[:],
                     waits=[("s", p.val("s"))], incs=[("v", 1)])
                ACTI(Pb[:], ps[:], ACT.Exp, bias=nmnew[:], scale=1.0,
                     accum=lsum_t[:],
                     waits=[("v", p.val("v")), ("t", pb_free_t[0])],
                     incs=[("s", 1)])
                s_bank_free[pi % 3] = (p.val("v"), p.val("s"))
                STT("vector", lrun[:, Q:Q + 1], lrun[:, Q:Q + 1], fsc_s[pi][:],
                    lsum_t[:], ALU.mult, ALU.add,
                    waits=[("s", p.val("s"))], incs=[("v", 1)])

            def emit_TRG_mm(pi):
                t_w = p.val("t")
                for j in range(4):
                    TRG(pB[:, j * 128:(j + 1) * 128],
                        Pb[:, j * 128:(j + 1) * 128], c_id32[:],
                        start=(j == 0), stop=(j == 3),
                        waits=([("s", p.val("s")), ("t", t_w),
                                ("v", pt_bank_free[0])]
                               if j == 0 else ()),
                        incs=([("t", 1)] if j == 3 else ()))
                pb_free_t[0] = p.val("t")

            def emit_PT_evac(pi):
                COPY("vector", PTh[:], pB[:],
                     waits=[("t", p.val("t")), ("v", p.val("v"))],
                     incs=[("v", 1)])
                TT_("vector", PTl[:], pB[:], PTh[:], ALU.subtract,
                    waits=[("v", p.val("v"))], incs=[("v", 1)])
                pt_v = p.val("v")
                pt_bank_free[0] = pt_v
                pt_done_v[pi] = pt_v

            def emit_AV(pi, b):
                aTh_ = aT3[0][b % 2]
                aTl_ = aT3[1][b % 2]
                n = 0
                for dh in range(2):
                    for (lh, rh) in ((PTh, aTh_), (PTh, aTl_),
                                     (PTl, aTh_)):
                        for kc in range(4):
                            n += 1
                            MM(pO[:, dh * 512:(dh + 1) * 512],
                               lh[:, kc * 128:(kc + 1) * 128],
                               rh[:, kc, dh * 512:(dh + 1) * 512],
                               start=((n % 12) == 1), stop=((n % 12) == 0),
                               waits=([("v", pt_done_v[pi]), enc_T_ld[b]]
                                      if n == 1 else ()),
                               incs=([("t", 1)] if n == 24 else ()))
                av_done_t[pi] = p.val("t")

            def emit_merge(pi, qt, kb, qg, aO_init):
                STT("vector", aO[:, qt * D:(qt + 1) * D],
                    aO[:, qt * D:(qt + 1) * D], fsc_s[pi][:], pO[:],
                    ALU.mult, ALU.add,
                    waits=[("t", av_done_t[pi]), ("v", aO_init)],
                    incs=[("v", 1)])

            fsc_s = {}
            aF3 = [[b.ap().rearrange("p (c n) -> p c n", c=DCH) for b in aFhb],
                   [b.ap().rearrange("p (c n) -> p c n", c=DCH) for b in aFlb]]
            aT3 = [[b.ap().rearrange("p (c n) -> p c n", c=4) for b in aThb],
                   [b.ap().rearrange("p (c n) -> p c n", c=4) for b in aTlb]]
            blocks = [(qg, kb) for qg in range(NQG) for kb in range(qg + 1)]
            blk_mm_done = {}
            enc_ld = {}
            qld = {}

            blk_s_done = {}
            enc_T_ld = {}

            def prefetch_F(b):
                if b >= len(blocks):
                    return
                qg, kb = blocks[b]
                pe = b % 2
                se = "fA" if pe == 0 else "fB"
                w = [("t", blk_s_done.get(b - 2, 0))]
                if b == 0:
                    w += [("t", p.val("t")), ("v", p.val("v")),
                          ("dg", prepass_dg), ("dv0", prepass_dv0),
                          ("dv1", prepass_dv1)]
                p.dma("sync", aF3[0][pe], eFh3[:, :, kb * 512:(kb + 1) * 512],
                      se, waits=w)
                enc_ld[b] = (se, p.dma("sync", aF3[1][pe],
                                       eFl3[:, :, kb * 512:(kb + 1) * 512],
                                       se))

            def prefetch_T(b):
                if b >= len(blocks):
                    return
                qg, kb = blocks[b]
                pe = b % 2
                se = "eA" if pe == 0 else "eB"
                w = [("t", blk_mm_done.get(b - 2, 0))]
                if b == 0:
                    w += [("t", p.val("t")), ("v", p.val("v")),
                          ("dg", prepass_dg), ("dv0", prepass_dv0),
                          ("dv1", prepass_dv1)]
                p.dma("sync", aT3[0][pe], eThq[:, kb * 4:(kb + 1) * 4, :], se,
                      waits=w)
                enc_T_ld[b] = (se, p.dma("sync", aT3[1][pe],
                                         eTlq[:, kb * 4:(kb + 1) * 4, :],
                                         se))

            aO_inits = {}

            fin_ld = {}

            def issue_finalize_loads(qg):
                dvv = p.val("dv")
                for qt in range(QG):
                    Q = qg * QG + qt
                    se = "dr" if qt < 2 else "dr2"
                    v = p.dma("sync", xfb[qt][:], x1_t[Q], se,
                              waits=[("dv", dvv)] if qt % 2 == 0 else ())
                    if qt % 2 == 1:
                        fin_ld[(qg, qt - 1)] = (se, v)
                        fin_ld[(qg, qt)] = (se, v)

            def finalize(qg):
                for qt in range(QG):
                    Q = qg * QG + qt
                    RECIP("vector", linv[:], lrun[:, Q:Q + 1],
                          waits=[("v", p.val("v"))], incs=[("v", 1)])
                    STT("vector", x2t[:], aO[:, qt * D:(qt + 1) * D], linv[:],
                        xfb[qt][:], ALU.mult, ALU.add,
                        waits=[fin_ld[(qg, qt)], ("v", p.val("v")),
                               ("dv", p.val("dv"))], incs=[("v", 1)])
                    p.dma("gpsimd", x2_t[Q], x2t[:], "dv",
                          waits=[("v", p.val("v"))])

            flat = [(b, qg, kb, qt) for b, (qg, kb) in enumerate(blocks)
                    for qt in range(QG)]
            NP = len(flat)
            entered = set()

            def block_entry(b):
                if b in entered:
                    return
                entered.add(b)
                qg, kb = blocks[b]
                if kb == 0:
                    qo = qg * DCH * TT
                    p.dma("scalar", aQh[:],
                          qth2[:, qo:qo + DCH * TT], "ds",
                          waits=[("t", p.val("t")), ("qs0", prepass_qs0),
                                 ("qs1", prepass_qs1)])
                    qld[qg] = p.dma("scalar", aQl[:],
                                    qtl2[:, qo:qo + DCH * TT], "ds")

            def lead_S(j):
                if j >= NP:
                    return
                bj, qgj, kbj, qtj = flat[j]
                block_entry(bj)
                fsc_s[j] = fscb[j % 2]
                emit_S(j, qtj, 0, kbj, qgj, enc_ld[bj], qld[qgj], bj)

            def memset_aO(qg):
                MEMSET("vector", aO[:], 0.0,
                       waits=[("v", p.val("v")), ("t", p.val("t"))],
                       incs=[("v", 1)])
                aO_inits[qg] = p.val("v")

            def post_merge(j):
                bj, qgj, kbj, qtj = flat[j]
                emit_merge(j, qtj, kbj, qgj, aO_inits[qgj])
                if qtj == QG - 1 and kbj == qgj:
                    finalize(qgj)
                    if j + 1 < NP:
                        memset_aO(flat[j + 1][1])

            prefetch_F(0)
            prefetch_T(0)
            prefetch_F(1)
            prefetch_T(1)
            lead_S(0)
            memset_aO(0)
            emit_VSP(0, flat[0][3], 0, flat[0][2], flat[0][1])
            for i in range(NP):
                bi, qgi, kbi, qti = flat[i]
                emit_TRG_mm(i)
                if i > 0:
                    post_merge(i - 1)
                emit_PT_evac(i)
                lead_S(i + 1)
                if i + 2 < NP:
                    block_entry(flat[i + 2][0])
                if i + 1 < NP and flat[i + 1][3] == QG - 1:
                    bn = flat[i + 1][0]
                    blk_s_done[bn] = s_done_t[i + 1]
                    prefetch_F(bn + 2)
                emit_AV(i, bi)
                if qti == QG - 1:
                    blk_mm_done[bi] = p.val("t")
                    prefetch_T(bi + 2)
                    if kbi == qgi:
                        issue_finalize_loads(qgi)
                if i + 1 < NP:
                    bn, qgn, kbn, qtn = flat[i + 1]
                    emit_VSP(i + 1, qtn, 0, kbn, qgn)
            post_merge(NP - 1)

        ffn_call(x0, x1b.ap(), cidx=0, wload_gu=None,
                 pre_waits=[("dx", init_dx)], prep=prep_tiles)
        ffn1_o = p.val("o")

        if upto == "ffn1":
            for nt in range(NT):
                p.dma("gpsimd",
                      out.rearrange("(n q) d -> n q d", q=128)[nt],
                      x1b.ap().rearrange("(n q) d -> n q d", q=128)[nt],
                      "dg", waits=[("o", ffn1_o)] if nt == 0 else ())
            p.wait("gpsimd", [("dg", p.val("dg"))])
        else:
            attention()
            attn_v = p.val("v")
            attn_t = p.val("t")
            if upto == "attn":
                for nt in range(NT):
                    p.dma("gpsimd",
                          out.rearrange("(n q) d -> n q d", q=128)[nt],
                          x2b.ap().rearrange("(n q) d -> n q d", q=128)[nt],
                          "dg", waits=[("dv", p.val("dv"))] if nt == 0 else ())
                p.wait("gpsimd", [("dg", p.val("dg"))])
            else:
                p.dma("sync", WgT[:], wgsp.ap(), "dw",
                      waits=[("t", attn_t), ("v", attn_v)])
                p.dma("sync", WuT[:], wusp.ap(), "dw")
                wload2 = p.val("dw")
                ffn_call(x2b.ap(), out, cidx=1, wload_gu=wload2,
                         pre_waits=[("dv", p.val("dv")), ("dv0", p.val("dv0")),
                                    ("dv1", p.val("dv1")), ("ds", p.val("ds")),
                                    ("dg", p.val("dg")), ("dr", p.val("dr")),
                                    ("drg", p.val("drg")), ("dx", p.val("dx"))])
                p.wait("gpsimd", [("o", p.val("o"))])

        with nc.Block() as block:
            def emit_for(engine_name):
                def run(eng):
                    for fn, waits, incs in p.ops[engine_name]:
                        for s, v in waits:
                            eng.wait_ge(sems[s], v)
                        if fn is None:
                            continue
                        inst = fn(eng)
                        for s, n in incs:
                            inst.then_inc(sems[s], n)
                return run
            block.sync(emit_for("sync"))
            block.vector(emit_for("vector"))
            block.scalar(emit_for("scalar"))
            block.tensor(emit_for("tensor"))
            block.gpsimd(emit_for("gpsimd"))

    return nc


def _host_inputs(inputs, core):
    x = np.ascontiguousarray(np.asarray(inputs["L0_attn_encoding"])[core])
    m = {
        "x0": x.astype(np.float32, copy=False),
        "Rm": np.asarray(inputs["L0_attn_R"], dtype=np.float32),
        "wg": np.asarray(inputs["L0_ffn_w_gate"], dtype=np.float32),
        "wu": np.asarray(inputs["L0_ffn_w_up"], dtype=np.float32),
        "wd": np.asarray(inputs["L0_ffn_w_down"], dtype=np.float32),
        "clip_x": np.asarray(inputs["L0_ffn_clip_x"], dtype=np.float32),
        "clip_d": np.asarray(inputs["L0_ffn_clip_down"], dtype=np.float32),
        "id32": np.eye(128, dtype=np.float32),
        "id16": np.eye(128, dtype=np.float16),
        "idb16": np.eye(128).astype(ml_dtypes.bfloat16),
    }
    jj = np.arange(896)[None, :]
    pp = np.arange(128)[:, None]
    m["maskm"] = np.where(jj <= pp + 384, 0.0, -1e9).astype(np.float32)
    return m


def kernel(**inputs):
    upto = inputs.pop("_upto", "full")
    nc = build_program(upto=upto)
    in_maps = [_host_inputs(inputs, b) for b in range(NCORES)]
    res = run_bass_kernel_spmd(nc, in_maps, core_ids=list(range(NCORES)))
    return np.stack([np.asarray(res.results[b]["out"]) for b in range(NCORES)],
                    axis=0)
